# version 20
# baseline (speedup 1.0000x reference)
import os
import numpy as np
from contextlib import ExitStack

import concourse.bass as bass
import concourse.mybir as mybir
from concourse.bass import AP
from concourse.bass_utils import run_bass_kernel_spmd

F32 = mybir.dt.float32
BF16 = mybir.dt.bfloat16
AF = mybir.ActivationFunctionType
ALU = mybir.AluOpType
AX = mybir.AxisListType

ENGS = ("sync", "gpsimd", "vector", "scalar", "tensor")


def make_cfg(S=2048, E=1024, H=16, M=1024, NE=8, HF=256):
    D = 64
    cfg = dict(S=S, E=E, H=H, D=D, M=M, NE=NE, HF=HF)
    cfg["R"] = 2 * M - 1
    cfg["NT"] = S // 128
    cfg["EC"] = E // 128
    cfg["NP"] = H // 2
    cfg["WBLK"] = S + 127
    cfg["PL"] = min(512, S)
    cfg["HFC"] = HF // 128
    return cfg


class Prog:
    def __init__(self, nc, es):
        self.nc = nc
        self.es = es
        self.ops = {e: [] for e in ENGS}
        self.sem = {}
        self.cnt = {}
        self.waited = {e: {} for e in ENGS}
        self.wait_log = {}
        Prog.last = self

    def mksem(self, *names):
        for n in names:
            self.sem[n] = self.es.enter_context(self.nc.semaphore(n))
            self.cnt[n] = 0

    def do(self, eng, fn, inc=None, n=1):
        semh = None
        if inc is not None:
            self.cnt[inc] += n
            semh = self.sem[inc]

        def op(e, fn=fn, semh=semh, n=n):
            r = fn(e)
            if semh is not None:
                r.then_inc(semh, n)
        self.ops[eng].append(op)
        return (inc, self.cnt[inc]) if inc is not None else None

    def tok(self, sem):
        return (sem, self.cnt[sem])

    def wait(self, eng, token):
        if token is None:
            return
        sem, val = token
        if val <= 0 or self.waited[eng].get(sem, -1) >= val:
            return
        self.waited[eng][sem] = val
        h = self.sem[sem]
        import sys as _sys
        ln = _sys._getframe(1).f_lineno
        self.wait_log.setdefault(eng, []).append((sem, val, ln))
        self.ops[eng].append(lambda e, h=h, v=val: e.wait_ge(h, v))

    def vsync(self):
        self.cnt["v"] += 1
        h = self.sem["v"]
        self.ops["vector"].append(lambda e, h=h: e.sem_inc(h, 1))
        self.wait("vector", self.tok("v"))


def build_program(cfg):
    S, E, H, D, M, NE, HF = (cfg[k] for k in ("S", "E", "H", "D", "M", "NE", "HF"))
    R, NT, EC, NP, WBLK, PL, HFC = (cfg[k] for k in ("R", "NT", "EC", "NP", "WBLK", "PL", "HFC"))
    NPL = S // PL
    TPP = PL // 128
    NJ = (S + 511) // 512
    SW = min(512, S)
    HS = S // 2
    NJH = (HS + 511) // 512
    U1W = 4 * WBLK

    nc = bass.Bass("TRN2", target_bir_lowering=False)
    nc.detect_race_conditions = False

    dram = {}
    def din(name, shape):
        dram[name] = nc.dram_tensor(name, list(shape), F32, kind="ExternalInput").ap()
    din("xT", (E, S))
    din("Wq", (E, E)); din("Wk", (E, E)); din("Wv", (E, E)); din("Wo", (E, E))
    din("bq", (128, NP)); din("bk", (128, NP)); din("bv", (128, NP)); din("bo", (128, EC))
    din("gamma", (128, EC)); din("beta", (128, EC))
    din("eTx", (D, R + 2048)); din("erevTx", (D, R + 2048))
    din("identity", (128, 128)); din("onesm", (128, 128))
    din("WgT", (E, NE)); din("msel", (NE, NE * 128))
    din("lnsc", (NE * 128, EC)); din("lnbs", (NE * 128, EC))
    din("W1", (NE, E, HF)); din("b1c", (NE, 128, HFC))
    din("W2", (NE, HF, E)); din("b2", (NE, E))
    yT = nc.dram_tensor("yT", [E, S], F32, kind="ExternalOutput").ap()
    dbgA1 = None
    if os.environ.get("KDBG_A1"):
        dbgA1 = nc.dram_tensor("dbgA1", [128, 16 * 2048], mybir.dt.bfloat16, kind="ExternalOutput").ap()

    es = ExitStack()
    with es:
        def sbuf(name, shape, dt):
            return es.enter_context(nc.sbuf_tensor(name, list(shape), dt))

        A1 = sbuf("A1", [128, max(NT * S, 2 * EC * S)], BF16)
        A2 = sbuf("A2", [128, EC * S], BF16)
        A3 = sbuf("A3", [128, max(NP * S, NE * HFC * HS)], BF16)
        U1 = sbuf("U1", [128, U1W], BF16)
        A1f = A1.ap().bitcast(F32)
        U1f = U1.ap().bitcast(F32)
        xchunk = U1f[:, 0:2 * S]
        w2st = U1.ap()[:, 0:2 * NE * HFC * 128]
        eT_d = sbuf("eT_d", [128, R + 2048], BF16)
        erevT_d = sbuf("erevT_d", [128, R + 2048], BF16)
        identb = sbuf("identb", [128, 128], BF16)
        onesb = sbuf("onesb", [128, 64], BF16)
        identf = sbuf("identf", [128, 128], F32)
        onesf = sbuf("onesf", [128, 128], F32)
        mselb = sbuf("mselb", [8, NE * 128], BF16)
        qT_p = sbuf("qT_p", [128, S], BF16)
        kT_p = sbuf("kT_p", [128, S], BF16)
        vT_p = sbuf("vT_p", [128, S], BF16)
        vnat = sbuf("vnat", [128, NT * 65], BF16)
        Rpc = sbuf("Rpc", [128, 8 * PL], BF16)
        ychunk = Rpc.ap().bitcast(F32)[:, 0:2 * SW]
        sq_s = sbuf("sq_s", [128, 1024], F32)
        Mcol = sbuf("Mcol", [128, NT], F32)
        sc1 = sbuf("sc1", [1, 8], F32)
        cb = sbuf("cb", [128, 4], F32)
        mcol8 = sbuf("mcol8", [128, 8], F32)
        ROWS = sbuf("ROWS", [128, max(S, R)], F32)
        r_mu = ROWS[0:1, 0:S]
        r_rs = ROWS[32:33, 0:S]
        r_sv = ROWS[64:65, 0:S]
        r_mu2 = ROWS[96:97, 0:S]
        r_kn = ROWS[96:97, 0:S]
        r_emx = ROWS[96:97, 0:R]
        wst = sbuf("wst", [128, 3 * E], BF16)
        w1st = wst.ap()[:, 0:EC * HF]
        if S >= 2048:
            _o1 = 2 * WBLK + 2
            b2st = U1.ap()[0:8, _o1:_o1 + EC * 128]
        else:
            b2st = sbuf("b2st", [8, EC * 128], BF16).ap()[:, :]
        biascols = sbuf("biascols", [128, 3 * NP], F32)
        bocol = sbuf("bocol", [128, EC], F32)
        gcol = sbuf("gcol", [128, EC], F32)
        btcol = sbuf("btcol", [128, EC], F32)
        wgst = sbuf("wgst", [128, EC * NE], F32)
        if S >= 2048:
            _o2 = 2 * WBLK + 2 + EC * 128 + 2
            masks = U1.ap()[0:8, _o2:_o2 + S]
            _o3 = _o2 + S + 2
            lgch = U1.ap().bitcast(F32)[0:8, (_o3 + 1) // 2:(_o3 + 1) // 2 + SW]
        else:
            masks = sbuf("masks", [8, S], BF16).ap()[:, :]
            lgch = sbuf("lgch", [8, 512], F32).ap()[:, :]
        sccol = sbuf("sccol", [128, EC], F32)
        b1p = sbuf("b1p", [128, HFC], F32)
        lnbcol = sbuf("lnbcol", [128, EC], BF16)
        PS_A = es.enter_context(nc.psum_tensor("PS_A", [128, 1024], F32))
        PS_B = es.enter_context(nc.psum_tensor("PS_B", [128, 1024], F32))
        PS_C = es.enter_context(nc.psum_tensor("PS_C", [128, min(2048, max(S, 512))], F32))

        p = Prog(nc, es)
        p.mksem("g", "y", "t", "v", "s")
        GE, SY, VE, SE, TE = "gpsimd", "sync", "vector", "scalar", "tensor"

        for half in (0, 64):
            p.do(GE, lambda e, h=half: e.dma_start(out=eT_d[h:h + D, :], in_=dram["eTx"]), inc="g", n=16)
            p.do(GE, lambda e, h=half: e.dma_start(out=erevT_d[h:h + D, :], in_=dram["erevTx"]), inc="g", n=16)
        p.do(GE, lambda e: e.dma_start(out=identb[:], in_=dram["identity"]), inc="g", n=16)
        p.do(GE, lambda e: e.dma_start(out=onesb[:], in_=dram["onesm"][:, 0:64]), inc="g", n=16)
        p.do(GE, lambda e: e.dma_start(out=mselb[0:NE, :], in_=dram["msel"]), inc="g", n=16)
        p.do(SY, lambda e: e.dma_start(out=identf[:], in_=dram["identity"]), inc="y", n=16)
        p.do(SY, lambda e: e.dma_start(out=onesf[:], in_=dram["onesm"]), inc="y", n=16)
        for i, nm in enumerate(("bq", "bk", "bv")):
            p.do(SY, lambda e, s=dram[nm], i=i: e.dma_start(
                out=biascols[:, i * NP:(i + 1) * NP], in_=s), inc="y", n=16)
        p.do(SY, lambda e: e.dma_start(out=bocol[:], in_=dram["bo"]), inc="y", n=16)
        p.do(SY, lambda e: e.dma_start(out=gcol[:], in_=dram["gamma"]), inc="y", n=16)
        p.do(SY, lambda e: e.dma_start(out=btcol[:], in_=dram["beta"]), inc="y", n=16)
        tok_const_g = p.tok("g")
        tok_const_y = p.tok("y")

        p.wait(VE, tok_const_g)
        for i in range((R + 511) // 512):
            w = min(512, R - i * 512)
            p.do(VE, lambda e, i=i, w=w: e.tensor_tensor(
                out=sq_s[0:D, (i % 2) * 512:(i % 2) * 512 + w],
                in0=eT_d[0:D, 1024 + i * 512:1024 + i * 512 + w], in1=eT_d[0:D, 1024 + i * 512:1024 + i * 512 + w],
                op=ALU.mult), inc="v")
            vt = p.tok("v")
            p.wait(TE, vt); p.wait(TE, tok_const_y)
            p.do(TE, lambda e, i=i, w=w: e.matmul(PS_B[0:1, (i % 2) * 512:(i % 2) * 512 + w],
                 onesf[0:D, 0:1], sq_s[0:D, (i % 2) * 512:(i % 2) * 512 + w],
                 start=True, stop=True), inc="t")
            t = p.tok("t")
            p.wait(VE, t)
            p.do(VE, lambda e, i=i, w=w: e.tensor_copy(out=r_emx[0:1, i * 512:i * 512 + w],
                 in_=PS_B[0:1, (i % 2) * 512:(i % 2) * 512 + w]), inc="v")
            p.wait(TE, p.tok("v"))
        p.vsync()
        p.do(VE, lambda e: e.reduce_max(out=sc1[0:1, 0:1], in_=r_emx, axis=AX.X), inc="v")
        p.vsync()
        p.wait(SE, p.tok("v"))
        p.do(SE, lambda e: e.activation(out=sc1[0:1, 0:1], in_=sc1[0:1, 0:1], func=AF.Sqrt), inc="s")
        emx_tok = p.tok("s")

        xslot_reader = [None, None]
        def load_x_chunk(c, src):
            slot = c % 2
            rd = xslot_reader[slot]
            rds = () if rd is None else (rd if isinstance(rd[0], tuple) else (rd,))
            for t in rds:
                p.wait(SY, t)
            p.do(SY, lambda e, c=c, slot=slot, src=src: e.dma_start(
                out=xchunk[:, slot * S:(slot + 1) * S], in_=src[c * 128:(c + 1) * 128, :]), inc="y", n=16)
            return p.tok("y"), xchunk[:, slot * S:(slot + 1) * S], slot

        nsq = 4 if S >= 2048 else 2

        def sqap(sl, w):
            if sl < 2:
                return sq_s[:, sl * 512:sl * 512 + w]
            return Rpcf[:, 1024 + (sl - 2) * 512:1024 + (sl - 2) * 512 + w]

        def ln_stats(loader):
            prev_mm = None
            mm_slot = [None] * 4
            for c in range(EC):
                tk, apc, slot = loader(c)
                for i in range(NJ):
                    w = min(512, S - i * 512)
                    sl = i % nsq
                    if (c * NJ + i) % 2 == 0:
                        p.wait(SE, tk)
                        p.wait(SE, mm_slot[sl])
                        p.do(SE, lambda e, a=apc, i=i, w=w, sl=sl: e.activation(
                            out=sqap(sl, w), in_=a[:, i * 512:i * 512 + w],
                            func=AF.Square), inc="s")
                        stok = p.tok("s")
                    else:
                        p.wait(VE, tk)
                        p.wait(VE, mm_slot[sl])
                        p.do(VE, lambda e, a=apc, i=i, w=w, sl=sl: e.tensor_tensor(
                            out=sqap(sl, w), in0=a[:, i * 512:i * 512 + w],
                            in1=a[:, i * 512:i * 512 + w], op=ALU.mult), inc="v")
                        stok = p.tok("v")
                    p.wait(TE, tk); p.wait(TE, stok); p.wait(TE, tok_const_y)
                    p.do(TE, lambda e, a=apc, i=i, w=w, c=c: e.matmul(
                        PS_C[0:1, i * 512:i * 512 + w], onesf[:, 0:1],
                        a[:, i * 512:i * 512 + w], start=(c == 0), stop=(c == EC - 1)), inc="t")
                    p.do(TE, lambda e, i=i, w=w, c=c, sl=sl: e.matmul(
                        PS_C[32:33, i * 512:i * 512 + w], onesf[:, 0:1],
                        sqap(sl, w), start=(c == 0), stop=(c == EC - 1)), inc="t")
                    prev_mm = p.tok("t")
                    mm_slot[sl] = prev_mm
                if slot is not None:
                    xslot_reader[slot] = prev_mm
            return prev_mm

        def ln_rows(eps, comb=False):
            t = p.tok("t")
            p.wait(VE, t)
            p.do(VE, lambda e: e.tensor_scalar(out=r_mu, in0=PS_C[0:1, 0:S],
                 scalar1=1.0 / E, scalar2=None, op0=ALU.mult), inc="v")
            p.do(VE, lambda e: e.tensor_scalar(out=r_rs, in0=PS_C[32:33, 0:S],
                 scalar1=1.0 / E, scalar2=None, op0=ALU.mult), inc="v")
            p.vsync()
            p.do(VE, lambda e: e.tensor_tensor(out=PS_C[64:65, 0:S], in0=r_mu, in1=r_mu, op=ALU.mult), inc="v")
            p.do(VE, lambda e: e.tensor_tensor(out=r_rs, in0=r_rs, in1=PS_C[64:65, 0:S], op=ALU.subtract), inc="v")
            if comb:
                p.do(VE, lambda e: e.tensor_scalar(out=r_rs, in0=r_rs,
                     scalar1=1.0 + float(eps), scalar2=float(eps) * float(eps),
                     op0=ALU.mult, op1=ALU.add), inc="v")
            else:
                p.do(VE, lambda e: e.tensor_scalar(out=r_rs, in0=r_rs,
                     scalar1=float(eps), scalar2=None, op0=ALU.add), inc="v")
            vt = p.tok("v")
            p.wait(SE, vt)
            p.do(SE, lambda e: e.activation(out=r_rs, in_=r_rs, func=AF.Sqrt), inc="s")
            p.do(SE, lambda e: e.activation(out=r_rs, in_=r_rs, func=AF.Ln), inc="s")
            p.do(SE, lambda e: e.activation(out=r_rs, in_=r_rs, func=AF.Exp, scale=-1.0), inc="s")
            return p.tok("s")

        Rpcf = Rpc.ap().bitcast(F32)
        kT_pf = kT_p.ap().bitcast(F32)
        vT_pf = vT_p.ap().bitcast(F32)

        def muap(j, w):
            return Rpcf[:, j * 512:j * 512 + w]

        def rsap(j, w):
            if S <= 1024:
                return Rpcf[:, S + j * 512:S + j * 512 + w]
            return (kT_pf if j < 2 else vT_pf)[:, (j % 2) * 512:(j % 2) * 512 + w]

        def stage_rows(rows_tok):
            pvst = [None, None]
            for j in range(NJ):
                w = min(512, S - j * 512)
                sl = j % 2
                p.wait(TE, rows_tok); p.wait(TE, tok_const_y); p.wait(TE, pvst[sl])
                p.do(TE, lambda e, j=j, w=w, sl=sl: e.matmul(
                     PS_A[:, sl * 512:sl * 512 + w], onesf[0:1, :],
                     r_mu[0:1, j * 512:j * 512 + w], start=True, stop=True), inc="t")
                p.do(TE, lambda e, j=j, w=w, sl=sl: e.matmul(
                     PS_B[:, sl * 512:sl * 512 + w], onesf[32:33, :],
                     r_rs[0:1, j * 512:j * 512 + w], start=True, stop=True), inc="t")
                mt = p.tok("t")
                p.wait(VE, mt)
                p.do(VE, lambda e, j=j, w=w, sl=sl: e.tensor_copy(
                    out=muap(j, w), in_=PS_A[:, sl * 512:sl * 512 + w]), inc="v")
                p.do(VE, lambda e, j=j, w=w, sl=sl: e.tensor_copy(
                    out=rsap(j, w), in_=PS_B[:, sl * 512:sl * 512 + w]), inc="v")
                pvst[sl] = p.tok("v")
            return p.tok("v")

        def ln_apply(loader, out_ap_fn, stg):
            prev_ve = None
            for c in range(EC):
                tk, apc, slot = loader(c)
                for j in range(NJ):
                    w = min(512, S - j * 512)
                    p.wait(VE, tk); p.wait(VE, stg)
                    p.do(VE, lambda e, a=apc, j=j, w=w: e.tensor_tensor(
                        out=sq_s[:, 0:w], in0=a[:, j * 512:j * 512 + w],
                        in1=muap(j, w), op=ALU.subtract), inc="v")
                    p.do(VE, lambda e, c=c, j=j, w=w: e.tensor_tensor(
                        out=out_ap_fn(c, j, w), in0=sq_s[:, 0:w],
                        in1=rsap(j, w), op=ALU.mult), inc="v")
                    prev_ve = p.tok("v")
                if slot is not None:
                    xslot_reader[slot] = prev_ve
            return prev_ve

        ln_stats(lambda c: load_x_chunk(c, dram["xT"]))
        ln1_rows = ln_rows(1e-5)
        hsT_tok = ln_apply(lambda c: load_x_chunk(c, dram["xT"]),
                           lambda c, j, w: A2[:, c * S + j * 512:c * S + j * 512 + w],
                           stage_rows(ln1_rows))

        state = {"proj_read": None, "last_ctx_mm": None,
                 "last_exps": [None, None], "sl_ctx": [None] * 8}

        def projections(pair):
            p.wait(GE, state["proj_read"])
            wtoks = []
            for i, nm in enumerate(("Wq", "Wk", "Wv")):
                p.do(GE, lambda e, s=dram[nm], i=i, pr=pair: e.dma_start(
                    out=wst[:, i * E:(i + 1) * E],
                    in_=AP(s.tensor, pr * 128, [[E, 128], [128 * E, EC], [1, 128]])), inc="g", n=16)
                wtoks.append(p.tok("g"))
            outs = (qT_p, kT_p, vT_p)
            prev_ve = None
            pv = [None, None, None, None]
            for i in range(3):
                p.wait(TE, wtoks[i]); p.wait(TE, hsT_tok)
                for j in range(NJ):
                    w = min(512, S - j * 512)
                    pslot = (i * NJ + j) % 4
                    psl = (PS_B[:, 0:0 + w], PS_B[:, 512:512 + w],
                           PS_A[:, 0:0 + w], PS_A[:, 512:512 + w])[pslot]
                    p.wait(TE, pv[pslot])
                    for c in range(EC):
                        p.do(TE, lambda e, i=i, j=j, c=c, psl=psl, w=w: e.matmul(
                            psl,
                            wst[:, i * E + c * 128:i * E + c * 128 + 128],
                            A2[:, c * S + j * 512:c * S + j * 512 + w],
                            start=(c == 0), stop=(c == EC - 1)), inc="t")
                    mt = p.tok("t")
                    p.wait(VE, mt); p.wait(VE, tok_const_y)
                    if i == 0:
                        p.do(VE, lambda e, i=i, j=j, o=outs[i], psl=psl, pr=pair, w=w: e.tensor_scalar(
                            out=o[:, j * 512:j * 512 + w], in0=psl,
                            scalar1=biascols[:, i * NP + pr:i * NP + pr + 1], scalar2=0.125,
                            op0=ALU.add, op1=ALU.mult), inc="v")
                    else:
                        p.do(VE, lambda e, i=i, j=j, o=outs[i], psl=psl, pr=pair, w=w: e.tensor_scalar(
                            out=o[:, j * 512:j * 512 + w], in0=psl,
                            scalar1=biascols[:, i * NP + pr:i * NP + pr + 1], scalar2=None,
                            op0=ALU.add), inc="v")
                    prev_ve = p.tok("v")
                    pv[pslot] = prev_ve
            return prev_ve, p.tok("t")

        def head_attention(h, proj_tok):
            hb = 64 * (h % 2)
            pair = h // 2
            prev_ctx = state["last_ctx_mm"]

            p.wait(VE, proj_tok)
            if prev_ctx is not None:
                p.wait(VE, prev_ctx)
            p.do(VE, lambda e: e.memset(
                AP(vnat.ap().tensor, 64, [[NT * 65, 128], [65, NT]]), 1.0), inc="v")
            vz = p.tok("v")
            p.wait(VE, state.get("last_norm_t"))
            for i in range(NJ):
                w = min(512, S - i * 512)
                p.do(VE, lambda e, i=i, w=w: e.tensor_tensor(
                    out=A1[hb:hb + D, i * 512:i * 512 + w],
                    in0=kT_p[hb:hb + D, i * 512:i * 512 + w],
                    in1=kT_p[hb:hb + D, i * 512:i * 512 + w], op=ALU.mult), inc="v")
            for i in range(NJ):
                w = min(512, S - i * 512)
                p.do(VE, lambda e, i=i, w=w: e.tensor_tensor(
                    out=A1[hb:hb + D, S + i * 512:S + i * 512 + w],
                    in0=qT_p[hb:hb + D, i * 512:i * 512 + w],
                    in1=qT_p[hb:hb + D, i * 512:i * 512 + w], op=ALU.mult), inc="v")
            sq_tok = p.tok("v")
            p.wait(TE, vz); p.wait(TE, proj_tok); p.wait(TE, tok_const_g)
            idsl = identb[hb:hb + D, hb:hb + D]
            for rc in range(NT):
                p.do(TE, lambda e, rc=rc: e.matmul(
                    PS_B[:, rc * 64:rc * 64 + D],
                    vT_p[hb:hb + D, rc * 128:(rc + 1) * 128], idsl,
                    start=True, stop=True), inc="t")
            mt = p.tok("t")
            p.wait(VE, mt)
            p.do(VE, lambda e: e.tensor_copy(
                out=AP(vnat.ap().tensor, 0, [[NT * 65, 128], [65, NT], [1, D]]),
                in_=AP(PS_B.ap().tensor, 0, [[1024, 128], [64, NT], [1, D]])), inc="v")
            vnat_tok = p.tok("v")

            p.wait(TE, sq_tok)
            for i in range(NJ):
                w = min(512, S - i * 512)
                p.do(TE, lambda e, i=i, w=w: e.matmul(PS_C[0:1, i * 512:i * 512 + w],
                     onesb[hb:hb + D, 0:1], A1[hb:hb + D, i * 512:i * 512 + w],
                     start=True, stop=True), inc="t")
            kt = p.tok("t")
            p.wait(TE, state["last_exps"][1])
            for lt in range(NT):
                p.do(TE, lambda e, lt=lt: e.matmul(PS_A[:, 512 + lt:513 + lt],
                     A1[hb:hb + D, S + lt * 128:S + lt * 128 + 128], onesb[hb:hb + D, 0:1],
                     start=True, stop=True), inc="t")
            qn = p.tok("t")
            p.wait(VE, kt)
            p.do(VE, lambda e: e.tensor_copy(out=r_kn, in_=PS_C[0:1, 0:S]), inc="v")
            rkn_tok = p.tok("v")
            p.vsync()
            p.do(VE, lambda e: e.reduce_max(out=sc1[0:1, 1:2], in_=r_kn, axis=AX.X), inc="v")
            p.vsync()
            p.wait(SE, p.tok("v")); p.wait(SE, emx_tok)
            p.do(SE, lambda e: e.activation(out=sc1[0:1, 1:2], in_=sc1[0:1, 1:2], func=AF.Sqrt), inc="s")
            kmx = p.tok("s")
            p.wait(VE, kmx)
            p.do(VE, lambda e: e.tensor_tensor(out=sc1[0:1, 2:3], in0=sc1[0:1, 1:2], in1=sc1[0:1, 0:1], op=ALU.add), inc="v")
            p.do(VE, lambda e: e.tensor_tensor(out=sc1[0:1, 3:4], in0=sc1[0:1, 1:2], in1=sc1[0:1, 0:1], op=ALU.mult), inc="v")
            p.vsync()
            p.do(VE, lambda e: e.tensor_scalar(out=sc1[0:1, 2:3], in0=sc1[0:1, 2:3],
                 scalar1=1.01, scalar2=None, op0=ALU.mult), inc="v")
            p.do(VE, lambda e: e.tensor_scalar(out=sc1[0:1, 3:4], in0=sc1[0:1, 3:4],
                 scalar1=0.126, scalar2=0.5, op0=ALU.mult, op1=ALU.add), inc="v")
            p.vsync()
            c01 = p.tok("v")
            p.wait(TE, c01)
            p.do(TE, lambda e: e.matmul(PS_B[:, 0:1], onesf[0:1, :], sc1[0:1, 2:3], start=True, stop=True), inc="t")
            p.do(TE, lambda e: e.matmul(PS_B[:, 1:2], onesf[0:1, :], sc1[0:1, 3:4], start=True, stop=True), inc="t")
            cbt = p.tok("t")
            p.wait(VE, cbt)
            p.do(VE, lambda e: e.tensor_copy(out=cb[:, 0:2], in_=PS_B[:, 0:2]), inc="v")
            p.vsync()
            cb_tok = p.tok("v")
            p.wait(SE, qn)
            p.do(SE, lambda e: e.activation(out=Mcol[:, 0:NT], in_=PS_A[:, 512:512 + NT], func=AF.Sqrt), inc="s")
            sqt = p.tok("s")
            p.wait(VE, sqt); p.wait(VE, cb_tok)
            p.do(VE, lambda e: e.tensor_scalar(out=Mcol[:, 0:NT], in0=Mcol[:, 0:NT],
                 scalar1=cb[:, 0:1], scalar2=cb[:, 1:2], op0=ALU.mult, op1=ALU.add), inc="v")
            p.do(VE, lambda e: e.tensor_scalar(out=Mcol[:, 0:NT], in0=Mcol[:, 0:NT],
                 scalar1=-1.0, scalar2=None, op0=ALU.mult), inc="v")
            mcol_tok = p.tok("v")

            skew_toks = [None] * NT
            qd_evp = [sqt, cb_tok, rkn_tok, rkn_tok]
            npair = 4 if S >= 2048 else 2
            nch = (WBLK + 511) // 512
            pgc = [0]
            nuslot = 3 if NT > 3 else 2
            for lt in range(NT):
                slot = lt % nuslot
                base = slot * WBLK
                l0 = lt * 128
                c_lo = (M - 1) - 127 - l0
                ecol = 1024 + c_lo
                ev_se = None
                ev_ve = None
                ch = 0
                while ch < nch:
                    nsub = min(2, nch - ch)
                    bp = pgc[0] % npair
                    pgc[0] += 1
                    pst, poff = ((PS_A, 0), (PS_B, 0), (PS_C, 0), (PS_C, 1024))[bp]
                    ja = ch * 512
                    wtot = min(2 * 512, WBLK - ja) if nsub == 2 else min(512, WBLK - ja)
                    p.wait(TE, proj_tok); p.wait(TE, tok_const_g)
                    p.wait(TE, qd_evp[bp])
                    for sub in range(nsub):
                        jas = (ch + sub) * 512
                        ws = min(512, WBLK - jas)
                        p.do(TE, lambda e, jas=jas, ws=ws, pst=pst, poff=poff, sub=sub,
                             ecol=ecol, l0=l0: e.matmul(
                            pst[:, poff + sub * 512:poff + sub * 512 + ws],
                            qT_p[hb:hb + D, l0:l0 + 128],
                            erevT_d[hb:hb + D, ecol + jas:ecol + jas + ws],
                            start=True, stop=True), inc="t")
                    mt = p.tok("t")
                    if pgc[0] % 2 == 0:
                        p.wait(SE, mt); p.wait(SE, mcol_tok)
                        if lt >= nuslot:
                            p.wait(SE, skew_toks[lt - nuslot])
                        if prev_ctx is not None:
                            p.wait(SE, prev_ctx)
                        p.do(SE, lambda e, ja=ja, wtot=wtot, pst=pst, poff=poff, base=base, lt=lt: e.activation(
                            out=U1.ap()[:, base + ja:base + ja + wtot],
                            in_=pst[:, poff:poff + wtot], func=AF.Identity,
                            bias=Mcol[:, lt:lt + 1]), inc="s")
                        qd_evp[bp] = ev_se = p.tok("s")
                    else:
                        p.wait(VE, mt); p.wait(VE, mcol_tok)
                        if lt >= nuslot:
                            p.wait(VE, skew_toks[lt - nuslot])
                        if prev_ctx is not None:
                            p.wait(VE, prev_ctx)
                        p.do(VE, lambda e, ja=ja, wtot=wtot, pst=pst, poff=poff, base=base, lt=lt: e.tensor_scalar(
                            out=U1.ap()[:, base + ja:base + ja + wtot],
                            in0=pst[:, poff:poff + wtot], scalar1=Mcol[:, lt:lt + 1], scalar2=None,
                            op0=ALU.add), inc="v")
                        qd_evp[bp] = ev_ve = p.tok("v")
                    ch += nsub
                Q, sem_n = (GE, "g") if lt % 2 == 0 else (SY, "y")
                p.wait(Q, ev_se); p.wait(Q, ev_ve)
                if prev_ctx is not None:
                    p.wait(Q, prev_ctx)
                p.do(Q, lambda e, base=base, lt=lt: e.dma_start(
                    out=A1[:, lt * S:(lt + 1) * S],
                    in_=AP(U1.ap().tensor, base + 127, [[U1W - 1, 128], [1, S]])), inc=sem_n, n=16)
                skew_toks[lt] = p.tok(sem_n)
            relq_tok = p.tok("g")
            if dbgA1 is not None and h == 0:
                p.wait(GE, relq_tok)
                p.do(GE, lambda e: e.dma_start(out=dbgA1[:, 0:NT * S], in_=A1[:, 0:NT * S]), inc="g", n=16)
                relq_tok = p.tok("g")

            nchk = (WBLK + 511) // 512
            kd_ev = [qd_evp[1], qd_evp[1]]
            kd_read = [(None, None), (None, None)]
            st_free = [qd_evp[0], qd_evp[0]]
            sl_exp = [None] * 8
            sl_ctx = state["sl_ctx"]
            r_toks = {}
            kdcnt = [0]

            def st_ps(stslot):
                return PS_A[:, stslot * PL:(stslot + 1) * PL]

            def emit_kd(rc):
                flushc(1)
                slot = rc % 2
                if nuslot == 3:
                    base = (3 * WBLK) if slot == 0 else 0
                else:
                    base = 2 * WBLK + slot * WBLK
                r0 = rc * 128
                ecol = 1024 + (M - 1) - 127 - r0
                for ch in range(nchk):
                    ja = ch * 512
                    w = min(512, WBLK - ja)
                    bank = kdcnt[0] % 2
                    kdcnt[0] += 1
                    p.wait(TE, proj_tok)
                    p.wait(TE, kd_ev[bank])
                    p.do(TE, lambda e, ja=ja, w=w, ecol=ecol, r0=r0, bank=bank: e.matmul(
                        PS_B[:, bank * 512:bank * 512 + w],
                        kT_p[hb:hb + D, r0:r0 + 128],
                        eT_d[hb:hb + D, ecol + ja:ecol + ja + w],
                        start=True, stop=True), inc="t")
                    mt = p.tok("t")
                    p.wait(VE, mt)
                    if nuslot == 3 and rc == 1:
                        for zt in [max((t for t in range(NT) if t % 3 == 0 and t % 2 == par), default=None)
                                   for par in (0, 1)]:
                            if zt is not None:
                                p.wait(VE, skew_toks[zt])
                    p.wait(VE, kd_read[slot][0]); p.wait(VE, kd_read[slot][1])
                    p.do(VE, lambda e, ja=ja, w=w, base=base, bank=bank: e.tensor_scalar(
                        out=U1.ap()[:, base + ja:base + ja + w],
                        in0=PS_B[:, bank * 512:bank * 512 + w],
                        scalar1=0.125, scalar2=None, op0=ALU.mult), inc="v")
                    kd_ev[bank] = p.tok("v")
                kd_blk = p.tok("v")
                last = [None, None]
                for lp in range(NPL):
                    rslot = (rc * NPL + lp) % 8
                    Q, sem_n, qi = (SY, "y", 0) if lp % 2 == 0 else (GE, "g", 1)
                    p.wait(Q, kd_blk)
                    p.wait(Q, sl_ctx[rslot])
                    p.do(Q, lambda e, base=base, lp=lp, rslot=rslot: e.dma_start(
                        out=Rpc[:, rslot * PL:(rslot + 1) * PL],
                        in_=AP(U1.ap().tensor, base + 127 + lp * PL,
                               [[U1W - 1, 128], [1, PL]])), inc=sem_n, n=16)
                    last[qi] = p.tok(sem_n)
                    r_toks[(rc, lp)] = (last[qi], None)
                kd_read[slot] = (last[0], last[1])

            pending = []

            def flushc(n):
                while len(pending) > n:
                    pending.pop(0)()
            emit_kd(0)
            for rc in range(NT):
                for lp in range(NPL):
                    idx = rc * NPL + lp
                    sslot = idx % 2
                    rslot = idx % 8
                    r_tok = r_toks.pop((rc, lp))
                    p.wait(TE, skew_toks[lp * TPP + TPP - 1])
                    if TPP >= 2:
                        p.wait(TE, skew_toks[lp * TPP + TPP - 2])
                    p.wait(TE, st_free[sslot])
                    p.wait(TE, r_tok[0]); p.wait(TE, r_tok[1])
                    p.do(TE, lambda e, rc=rc, lp=lp, sslot=sslot: e.matmul(
                        st_ps(sslot),
                        kT_p[hb:hb + D, rc * 128:rc * 128 + 128],
                        qT_p[hb:hb + D, lp * PL:lp * PL + PL],
                        start=True, stop=False), inc="t")
                    for k in range(TPP):
                        lt = lp * TPP + k
                        p.do(TE, lambda e, rc=rc, lt=lt, sslot=sslot, k=k: e.matmul(
                            st_ps(sslot)[:, k * 128:(k + 1) * 128],
                            A1[:, lt * S + rc * 128:lt * S + rc * 128 + 128],
                            identb[:], start=False, stop=False), inc="t")
                    p.do(TE, lambda e, sslot=sslot, rslot=rslot: e.matmul(
                        st_ps(sslot), identb[:],
                        Rpc[:, rslot * PL:(rslot + 1) * PL],
                        start=False, stop=True), inc="t")
                    st_mm = p.tok("t")
                    flushc(1)
                    p.wait(SE, st_mm)
                    p.do(SE, lambda e, sslot=sslot, rslot=rslot: e.activation(
                        out=Rpc[:, rslot * PL:(rslot + 1) * PL],
                        in_=st_ps(sslot), func=AF.Exp), inc="s")
                    sl_exp[rslot] = p.tok("s")
                    st_free[sslot] = sl_exp[rslot]

                    def mk_ctx(rc=rc, lp=lp, rslot=rslot, exp_tok=sl_exp[rslot]):
                        def doit():
                            p.wait(TE, exp_tok); p.wait(TE, vnat_tok)
                            p.do(TE, lambda e: e.matmul(
                                PS_C[0:65, lp * PL:lp * PL + PL],
                                vnat[:, rc * 65:(rc + 1) * 65],
                                Rpc[:, rslot * PL:(rslot + 1) * PL],
                                start=(rc == 0), stop=(rc == NT - 1)), inc="t")
                            sl_ctx[rslot] = p.tok("t")
                        return doit
                    pending.append(mk_ctx())
                    if lp == min(0, NPL - 1) and rc + 1 < NT:
                        emit_kd(rc + 1)
            flushc(0)
            ctx_mm = p.tok("t")
            state["last_ctx_mm"] = ctx_mm
            state["last_exps"] = [st_free[0], st_free[1]]

            p.wait(SE, ctx_mm)
            p.do(SE, lambda e: e.activation(out=r_sv, in_=PS_C[64:65, 0:S], func=AF.Ln), inc="s")
            p.do(SE, lambda e: e.activation(out=A1[64:65, 0:S], in_=r_sv, func=AF.Exp, scale=-1.0), inc="s")
            rsum_tok = p.tok("s")
            for j in range(NJ):
                w = min(512, S - j * 512)
                p.wait(TE, rsum_tok)
                p.do(TE, lambda e, j=j, w=w: e.matmul(PS_A[0:D, 0:w], onesb[64:65, 0:D],
                     A1[64:65, j * 512:j * 512 + w], start=True, stop=True), inc="t")
                bt = p.tok("t")
                p.wait(VE, bt)
                p.do(VE, lambda e, j=j, w=w: e.tensor_copy(
                    out=sq_s[0:D, 0:w], in_=PS_C[0:D, j * 512:j * 512 + w]), inc="v")
                p.do(VE, lambda e, j=j, w=w, pair=pair: e.tensor_tensor(
                    out=A3[hb:hb + D, pair * S + j * 512:pair * S + j * 512 + w],
                    in0=sq_s[0:D, 0:w], in1=PS_A[0:D, 0:w], op=ALU.mult), inc="v")
                p.wait(TE, p.tok("v"))
            state["last_norm_t"] = p.tok("t")
            return p.tok("v")

        ctx_done = []
        for pr in range(NP):
            ptok, wread = projections(pr)
            state["proj_read"] = wread
            ctx_done.append(head_attention(2 * pr, ptok))
            ctx_done.append(head_attention(2 * pr + 1, ptok))

        for c in ctx_done:
            p.wait(TE, c)
        skl = (p.tok("y"), p.tok("g"))
        xslot_reader[0] = skl
        xslot_reader[1] = skl
        h_ev = [None, None, None, None]
        wo_rd = [None, None]
        for ec in range(EC):
            wslot = ec % 2
            p.wait(GE, state["proj_read"])
            p.wait(GE, wo_rd[wslot])
            p.do(GE, lambda e, ec=ec, wslot=wslot: e.dma_start(
                out=wst[:, wslot * E:(wslot + 1) * E],
                in_=AP(dram["Wo"].tensor, ec * 128, [[E, 128], [128 * E, EC], [1, 128]])), inc="g", n=16)
            wo_ld = p.tok("g")
            for j in range(NJ):
                w = min(512, S - j * 512)
                pslot = (ec * NJ + j) % 4
                psl = (PS_B[:, 0:0 + w], PS_B[:, 512:512 + w],
                       PS_A[:, 0:0 + w], PS_A[:, 512:512 + w])[pslot]
                p.wait(TE, wo_ld)
                p.wait(TE, h_ev[pslot])
                for kc in range(EC):
                    p.do(TE, lambda e, ec=ec, j=j, kc=kc, psl=psl, w=w, wslot=wslot: e.matmul(
                        psl,
                        wst[:, wslot * E + kc * 128:wslot * E + kc * 128 + 128],
                        A3[:, kc * S + j * 512:kc * S + j * 512 + w],
                        start=(kc == 0), stop=(kc == EC - 1)), inc="t")
                mt = p.tok("t")
                p.wait(VE, mt); p.wait(VE, hsT_tok); p.wait(VE, tok_const_y)
                p.do(VE, lambda e, ec=ec, psl=psl, w=w: e.tensor_scalar(
                    out=sq_s[:, 0:w], in0=psl,
                    scalar1=bocol[:, ec:ec + 1], scalar2=None, op0=ALU.add), inc="v")
                p.do(VE, lambda e, ec=ec, j=j, w=w: e.tensor_tensor(
                    out=A1f[:, ec * S + j * 512:ec * S + j * 512 + w],
                    in0=sq_s[:, 0:w], in1=A2[:, ec * S + j * 512:ec * S + j * 512 + w],
                    op=ALU.add), inc="v")
                h_ev[pslot] = p.tok("v")
            wo_rd[wslot] = p.tok("t")
        h_tok = p.tok("v")

        ln_stats(lambda c: (h_tok, A1f[:, c * S:(c + 1) * S], None))
        hrows = ln_rows(1e-12)
        stg3 = stage_rows(hrows)
        prev_ve = None
        for c in range(EC):
            xt, xap, slot = load_x_chunk(c, dram["xT"])
            for j in range(NJ):
                w = min(512, S - j * 512)
                p.wait(VE, stg3); p.wait(VE, xt)
                p.do(VE, lambda e, c=c, j=j, w=w: e.tensor_tensor(
                    out=sq_s[:, 0:w], in0=A1f[:, c * S + j * 512:c * S + j * 512 + w],
                    in1=muap(j, w), op=ALU.subtract), inc="v")
                p.do(VE, lambda e, j=j, w=w: e.tensor_tensor(
                    out=sq_s[:, 0:w], in0=sq_s[:, 0:w],
                    in1=rsap(j, w), op=ALU.mult), inc="v")
                p.do(VE, lambda e, c=c, w=w: e.scalar_tensor_tensor(
                    out=sq_s[:, 0:w], in0=sq_s[:, 0:w], scalar=gcol[:, c:c + 1],
                    in1=btcol[:, c:c + 1].to_broadcast([128, w]), op0=ALU.mult, op1=ALU.add), inc="v")
                p.do(VE, lambda e, c=c, j=j, w=w, slot=slot: e.tensor_tensor(
                    out=A1f[:, c * S + j * 512:c * S + j * 512 + w],
                    in0=sq_s[:, 0:w],
                    in1=xchunk[:, slot * S + j * 512:slot * S + j * 512 + w], op=ALU.add), inc="v")
                prev_ve = p.tok("v")
            xslot_reader[slot] = prev_ve
        x2_tok = prev_ve

        ln_stats(lambda c: (x2_tok, A1f[:, c * S:(c + 1) * S], None))
        ln2rows = ln_rows(1e-5, comb=True)
        stg2 = stage_rows(ln2rows)

        p.do(GE, lambda e: e.dma_start(
            out=wgst[:, :],
            in_=AP(dram["WgT"].tensor, 0, [[NE, 128], [128 * NE, EC], [1, NE]])), inc="g", n=16)
        wg_tok = p.tok("g")
        prev_gt = [None, None]
        lg_guard = None
        prev_mv = None
        for j in range(NJ):
            w = min(512, S - j * 512)
            for c in range(EC):
                zsl = c % 2
                p.wait(VE, stg2); p.wait(VE, x2_tok)
                p.wait(VE, prev_gt[zsl])
                p.do(VE, lambda e, c=c, j=j, w=w, zsl=zsl: e.tensor_tensor(
                    out=sq_s[:, zsl * 512:zsl * 512 + w],
                    in0=A1f[:, c * S + j * 512:c * S + j * 512 + w],
                    in1=muap(j, w), op=ALU.subtract), inc="v")
                z_ck = p.tok("v")
                p.wait(TE, z_ck); p.wait(TE, wg_tok)
                if c == 0:
                    p.wait(TE, lg_guard)
                p.do(TE, lambda e, c=c, w=w, zsl=zsl: e.matmul(
                    PS_B[0:NE, 0:w], wgst[:, c * NE:(c + 1) * NE],
                    sq_s[:, zsl * 512:zsl * 512 + w],
                    start=(c == 0), stop=(c == EC - 1)), inc="t")
                prev_gt[zsl] = p.tok("t")
            gt_last = p.tok("t")
            p.wait(VE, gt_last)
            p.do(VE, lambda e, w=w: e.tensor_copy(out=lgch[0:NE, 0:w], in_=PS_B[0:NE, 0:w]), inc="v")
            p.vsync()
            for ti in range(w // 128):
                lt = (j * 512) // 128 + ti
                p.wait(TE, p.tok("v")); p.wait(TE, tok_const_y)
                p.do(TE, lambda e, ti=ti: e.matmul(PS_B[0:128, 512:512 + NE],
                     lgch[0:NE, ti * 128:(ti + 1) * 128], identf[0:NE, 0:NE], start=True, stop=True), inc="t")
                tt = p.tok("t")
                p.wait(VE, tt)
                p.do(VE, lambda e: e.tensor_copy(out=mcol8[:, 0:NE], in_=PS_B[:, 512:512 + NE]), inc="v")
                p.vsync()
                p.do(VE, lambda e: e.reduce_max(out=cb[:, 2:3], in_=mcol8[:, 0:NE], axis=AX.X), inc="v")
                p.vsync()
                p.do(VE, lambda e: e.tensor_scalar(out=mcol8[:, 0:NE], in0=mcol8[:, 0:NE],
                     scalar1=cb[:, 2:3], scalar2=None, op0=ALU.is_ge), inc="v")
                p.do(VE, lambda e: e.memset(cb[:, 3:4], 0.0), inc="v")
                p.vsync()
                p.do(VE, lambda e: e.tensor_tensor_scan(out=sq_s[:, 0:NE], data0=mcol8[:, 0:NE],
                     data1=cb[:, 3:4].to_broadcast([128, NE]), initial=0.0, op0=ALU.add, op1=ALU.add), inc="v")
                p.vsync()
                p.do(VE, lambda e: e.tensor_scalar(out=sq_s[:, 0:NE], in0=sq_s[:, 0:NE],
                     scalar1=1.0, scalar2=None, op0=ALU.is_equal), inc="v")
                p.do(VE, lambda e: e.tensor_tensor(out=mcol8[:, 0:NE], in0=mcol8[:, 0:NE],
                     in1=sq_s[:, 0:NE], op=ALU.mult), inc="v")
                p.vsync()
                mcl = p.tok("v")
                p.wait(TE, mcl)
                p.do(TE, lambda e: e.matmul(PS_B[0:NE, 640:768],
                     mcol8[:, 0:NE], identf[:, 0:128], start=True, stop=True), inc="t")
                tb = p.tok("t")
                p.wait(VE, tb)
                p.do(VE, lambda e, lt=lt: e.tensor_copy(out=masks[0:NE, lt * 128:(lt + 1) * 128],
                     in_=PS_B[0:NE, 640:768]), inc="v")
                prev_mv = p.tok("v")
                p.wait(TE, prev_mv)
            lg_guard = p.tok("v")
        masks_tok = prev_mv
        xhat_tok = ln_apply(lambda c: (x2_tok, A1f[:, c * S:(c + 1) * S], None),
                            lambda c, j, w: A2[:, c * S + j * 512:c * S + j * 512 + w], stg2)

        p.wait(GE, masks_tok)
        p.do(GE, lambda e: e.dma_start(
            out=b2st[0:NE, :], in_=AP(dram["b2"].tensor, 0, [[E, NE], [1, E]])), inc="g", n=16)
        b2tok = p.tok("g")

        hm_last = None
        w1_read = None
        prev_b1tok = None
        prev_b1_user = None
        w2_rd = [None, None]
        ych_rd = [None, None]
        ffn_first = True
        for half in range(2):
            hoff = half * HS
            for ex in range(NE):
                p.wait(GE, w1_read); p.wait(GE, masks_tok)
                if ffn_first:
                    p.wait(GE, xhat_tok)
                    p.wait(GE, wo_rd[0]); p.wait(GE, wo_rd[1])
                p.do(GE, lambda e, ex=ex: e.dma_start(
                    out=w1st,
                    in_=AP(dram["W1"].tensor, ex * E * HF, [[HF, 128], [128 * HF, EC], [1, HF]])), inc="g", n=16)
                p.do(GE, lambda e, ex=ex: e.dma_start(
                    out=sccol[:, :], in_=dram["lnsc"][ex * 128:(ex + 1) * 128, :]), inc="g", n=16)
                p.do(GE, lambda e, ex=ex: e.dma_start(
                    out=lnbcol[:, :], in_=dram["lnbs"][ex * 128:(ex + 1) * 128, :]), inc="g", n=16)
                w1tok = p.tok("g")
                p.wait(TE, w1tok); p.wait(TE, prev_b1tok)
                for mc in range(HFC):
                    for c in range(EC):
                        p.do(TE, lambda e, mc=mc, c=c: e.matmul(
                            PS_C[0:128, mc:mc + 1],
                            w1st[:, c * HF + mc * 128:c * HF + (mc + 1) * 128],
                            lnbcol[:, c:c + 1], start=(c == 0), stop=(c == EC - 1)), inc="t")
                b1mm = p.tok("t")
                p.wait(VE, w1tok); p.wait(VE, b1mm)
                if hm_last is not None:
                    p.wait(VE, hm_last)
                for c in range(EC):
                    p.do(VE, lambda e, c=c: e.tensor_scalar(
                        out=w1st[:, c * HF:(c + 1) * HF], in0=w1st[:, c * HF:(c + 1) * HF],
                        scalar1=sccol[:, c:c + 1], scalar2=None, op0=ALU.mult), inc="v")
                w1s_tok = p.tok("v")
                p.wait(SY, prev_b1_user)
                p.do(SY, lambda e, ex=ex: e.dma_start(
                    out=b1p[:, 0:HFC],
                    in_=AP(dram["b1c"].tensor, ex * 128 * HFC, [[HFC, 128], [1, HFC]])), inc="y", n=16)
                b1ld = p.tok("y")
                p.wait(VE, b1mm); p.wait(VE, b1ld)
                p.do(VE, lambda e: e.tensor_tensor(out=b1p[:, 0:HFC], in0=b1p[:, 0:HFC],
                     in1=PS_C[:, 0:HFC], op=ALU.add), inc="v")
                p.vsync()
                b1tok = p.tok("v")
                prev_b1tok = b1tok
                hbase = ex * HFC * HS
                he_mm = [None, None]
                mb_rd = [None, None]
                for mc in range(HFC):
                    for j in range(NJH):
                        w = min(512, HS - j * 512)
                        pslot = (mc * NJH + j) % 2
                        gj = hoff + j * 512
                        p.wait(TE, xhat_tok); p.wait(TE, w1s_tok)
                        p.wait(TE, he_mm[pslot])
                        for c in range(EC):
                            p.do(TE, lambda e, mc=mc, gj=gj, c=c, pslot=pslot, w=w: e.matmul(
                                PS_A[:, pslot * 512:pslot * 512 + w],
                                w1st[:, c * HF + mc * 128:c * HF + (mc + 1) * 128],
                                A2[:, c * S + gj:c * S + gj + w],
                                start=(c == 0), stop=(c == EC - 1)), inc="t")
                        p.wait(TE, masks_tok); p.wait(TE, tok_const_g)
                        p.wait(TE, mb_rd[pslot])
                        p.do(TE, lambda e, ex=ex, gj=gj, pslot=pslot, w=w: e.matmul(
                            PS_B[:, pslot * 512:pslot * 512 + w],
                            mselb[0:NE, ex * 128:(ex + 1) * 128],
                            masks[0:NE, gj:gj + w], start=True, stop=True), inc="t")
                        hmm = p.tok("t")
                        p.wait(SE, hmm); p.wait(SE, b1tok)
                        if mb_rd[pslot] is not None:
                            p.wait(SE, mb_rd[pslot])
                        p.do(SE, lambda e, mc=mc, pslot=pslot, w=w: e.activation(
                            out=sq_s[:, pslot * 512:pslot * 512 + w], in_=PS_A[:, pslot * 512:pslot * 512 + w],
                            func=AF.Gelu, bias=b1p[:, mc:mc + 1]), inc="s")
                        ge_tok = p.tok("s")
                        he_mm[pslot] = ge_tok
                        prev_b1_user = ge_tok
                        p.wait(VE, ge_tok)
                        if ffn_first:
                            p.wait(VE, wo_rd[0]); p.wait(VE, wo_rd[1])
                        p.do(VE, lambda e, mc=mc, j=j, pslot=pslot, w=w, hbase=hbase: e.tensor_tensor(
                            out=A3[:, hbase + mc * HS + j * 512:hbase + mc * HS + j * 512 + w],
                            in0=sq_s[:, pslot * 512:pslot * 512 + w], in1=PS_B[:, pslot * 512:pslot * 512 + w],
                            op=ALU.mult), inc="v")
                        mb_rd[pslot] = p.tok("v")
                        hm_last = mb_rd[pslot]
                        ffn_first = False
                w1_read = p.tok("t")
            hall_tok = hm_last

            for ec in range(EC):
                wslot = ec % 2
                p.wait(GE, w2_rd[wslot]); p.wait(GE, hall_tok)
                p.do(GE, lambda e, ec=ec, wslot=wslot: e.dma_start(
                    out=w2st[:, wslot * NE * HFC * 128:(wslot + 1) * NE * HFC * 128],
                    in_=AP(dram["W2"].tensor, ec * 128, [[E, 128], [HF * E, NE], [128 * E, HFC], [1, 128]])), inc="g", n=16)
                w2ld = p.tok("g")
                for j in range(NJH):
                    w = min(512, HS - j * 512)
                    gj = hoff + j * 512
                    pslot = j % 2
                    p.wait(TE, hall_tok); p.wait(TE, w2ld); p.wait(TE, b2tok); p.wait(TE, masks_tok)
                    p.wait(TE, ych_rd[pslot])
                    first = True
                    for ex in range(NE):
                        hbase = ex * HFC * HS
                        for kc in range(HFC):
                            p.do(TE, lambda e, j=j, ex=ex, kc=kc, pslot=pslot, w=w, first=first,
                                 hbase=hbase, wslot=wslot: e.matmul(
                                PS_A[:, pslot * 512:pslot * 512 + w],
                                w2st[:, wslot * NE * HFC * 128 + (ex * HFC + kc) * 128:
                                     wslot * NE * HFC * 128 + (ex * HFC + kc) * 128 + 128],
                                A3[:, hbase + kc * HS + j * 512:hbase + kc * HS + j * 512 + w],
                                start=first, stop=False), inc="t")
                            first = False
                    p.do(TE, lambda e, ec=ec, gj=gj, pslot=pslot, w=w: e.matmul(
                        PS_A[:, pslot * 512:pslot * 512 + w],
                        b2st[0:NE, ec * 128:(ec + 1) * 128],
                        masks[0:NE, gj:gj + w], start=False, stop=True), inc="t")
                    ymm = p.tok("t")
                    p.wait(VE, ymm); p.wait(VE, x2_tok)
                    p.wait(VE, ych_rd[pslot])
                    p.do(VE, lambda e, ec=ec, gj=gj, pslot=pslot, w=w: e.tensor_tensor(
                        out=ychunk[:, pslot * SW:pslot * SW + w],
                        in0=PS_A[:, pslot * 512:pslot * 512 + w],
                        in1=A1f[:, ec * S + gj:ec * S + gj + w], op=ALU.add), inc="v")
                    yev = p.tok("v")
                    p.wait(SY, yev)
                    p.do(SY, lambda e, ec=ec, gj=gj, pslot=pslot, w=w: e.dma_start(
                        out=yT[ec * 128:(ec + 1) * 128, gj:gj + w],
                        in_=ychunk[:, pslot * SW:pslot * SW + w]), inc="y", n=16)
                    ych_rd[pslot] = p.tok("y")
                w2_rd[wslot] = p.tok("t")
        p.wait(SY, p.tok("y"))

        block = es.enter_context(nc.Block())

        @block.sync
        def _(sync):
            for f in p.ops["sync"]:
                f(sync)

        @block.gpsimd
        def _(gpsimd):
            for f in p.ops["gpsimd"]:
                f(gpsimd)

        @block.vector
        def _(vector):
            for f in p.ops["vector"]:
                f(vector)

        @block.scalar
        def _(scalar):
            for f in p.ops["scalar"]:
                f(scalar)

        @block.tensor
        def _(tensor):
            for f in p.ops["tensor"]:
                f(tensor)

    return nc


def host_inputs(cfg, inputs, b):
    S, E, H, M, NE, HF = (cfg[k] for k in ("S", "E", "H", "M", "NE", "HF"))
    x = np.ascontiguousarray(np.asarray(inputs["L0_attn_hidden_states"])[b].T)
    d = {
        "xT": x,
        "Wq": np.ascontiguousarray(inputs["L0_attn_Wq"]),
        "Wk": np.ascontiguousarray(inputs["L0_attn_Wk"]),
        "Wv": np.ascontiguousarray(inputs["L0_attn_Wv"]),
        "Wo": np.ascontiguousarray(inputs["L0_attn_Wo"]),
        "bq": np.ascontiguousarray(np.asarray(inputs["L0_attn_bq"]).reshape(-1, 128).T),
        "bk": np.ascontiguousarray(np.asarray(inputs["L0_attn_bk"]).reshape(-1, 128).T),
        "bv": np.ascontiguousarray(np.asarray(inputs["L0_attn_bv"]).reshape(-1, 128).T),
        "bo": np.ascontiguousarray(np.asarray(inputs["L0_attn_bo"]).reshape(-1, 128).T),
        "gamma": np.ascontiguousarray(np.asarray(inputs["L0_attn_ln_gamma"]).reshape(-1, 128).T),
        "beta": np.ascontiguousarray(np.asarray(inputs["L0_attn_ln_beta"]).reshape(-1, 128).T),
        "eTx": np.ascontiguousarray(np.pad(np.asarray(inputs["L0_attn_dist_emb"]).T, ((0, 0), (1024, 1024)), mode="edge")),
        "erevTx": np.ascontiguousarray(np.pad(np.asarray(inputs["L0_attn_dist_emb"])[::-1].T, ((0, 0), (1024, 1024)), mode="edge")),
        "identity": np.eye(128, dtype=np.float32),
        "onesm": np.ones((128, 128), dtype=np.float32),
        "msel": np.repeat(np.eye(NE, dtype=np.float32), 128, axis=1),
        "WgT": np.ascontiguousarray(np.asarray(inputs["L0_ffn_Wg"]).T),
        "lnsc": np.ascontiguousarray(np.asarray(inputs["L0_ffn_ln_scale"]).reshape(NE, -1, 128).transpose(0, 2, 1).reshape(NE * 128, -1)),
        "lnbs": np.ascontiguousarray(np.asarray(inputs["L0_ffn_ln_bias"]).reshape(NE, -1, 128).transpose(0, 2, 1).reshape(NE * 128, -1)),
        "W1": np.ascontiguousarray(inputs["L0_ffn_W1"]),
        "b1c": np.ascontiguousarray(
            np.asarray(inputs["L0_ffn_b1"]).reshape(NE, HF // 128, 128).transpose(0, 2, 1)),
        "W2": np.ascontiguousarray(inputs["L0_ffn_W2"]),
        "b2": np.ascontiguousarray(inputs["L0_ffn_b2"]),
    }
    return {k: np.asarray(v, dtype=np.float32) for k, v in d.items()}


def kernel(**inputs):
    cfg = make_cfg()
    nc = build_program(cfg)
    in_maps = [host_inputs(cfg, inputs, b) for b in range(8)]
    res = run_bass_kernel_spmd(nc, in_maps, core_ids=list(range(8)))
    outs = [np.asarray(r["yT"]).T for r in res.results]
    return np.stack(outs, axis=0).astype(np.float32)


# version 21
# speedup vs baseline: 1.0120x; 1.0120x over previous
import os
import numpy as np
from contextlib import ExitStack

import concourse.bass as bass
import concourse.mybir as mybir
from concourse.bass import AP
from concourse.bass_utils import run_bass_kernel_spmd

F32 = mybir.dt.float32
BF16 = mybir.dt.bfloat16
AF = mybir.ActivationFunctionType
ALU = mybir.AluOpType
AX = mybir.AxisListType

ENGS = ("sync", "gpsimd", "vector", "scalar", "tensor")


def make_cfg(S=2048, E=1024, H=16, M=1024, NE=8, HF=256):
    D = 64
    cfg = dict(S=S, E=E, H=H, D=D, M=M, NE=NE, HF=HF)
    cfg["R"] = 2 * M - 1
    cfg["NT"] = S // 128
    cfg["EC"] = E // 128
    cfg["NP"] = H // 2
    cfg["WBLK"] = S + 127
    cfg["PL"] = min(512, S)
    cfg["HFC"] = HF // 128
    return cfg


class Prog:
    def __init__(self, nc, es):
        self.nc = nc
        self.es = es
        self.ops = {e: [] for e in ENGS}
        self.sem = {}
        self.cnt = {}
        self.waited = {e: {} for e in ENGS}
        self.wait_log = {}
        Prog.last = self

    def mksem(self, *names):
        for n in names:
            self.sem[n] = self.es.enter_context(self.nc.semaphore(n))
            self.cnt[n] = 0

    def do(self, eng, fn, inc=None, n=1):
        semh = None
        if inc is not None:
            self.cnt[inc] += n
            semh = self.sem[inc]

        def op(e, fn=fn, semh=semh, n=n):
            r = fn(e)
            if semh is not None:
                r.then_inc(semh, n)
        self.ops[eng].append(op)
        return (inc, self.cnt[inc]) if inc is not None else None

    def tok(self, sem):
        return (sem, self.cnt[sem])

    def wait(self, eng, token):
        if token is None:
            return
        sem, val = token
        if val <= 0 or self.waited[eng].get(sem, -1) >= val:
            return
        self.waited[eng][sem] = val
        h = self.sem[sem]
        import sys as _sys
        ln = _sys._getframe(1).f_lineno
        self.wait_log.setdefault(eng, []).append((sem, val, ln))
        self.ops[eng].append(lambda e, h=h, v=val: e.wait_ge(h, v))

    def vsync(self):
        self.cnt["v"] += 1
        h = self.sem["v"]
        self.ops["vector"].append(lambda e, h=h: e.sem_inc(h, 1))
        self.wait("vector", self.tok("v"))


def build_program(cfg):
    S, E, H, D, M, NE, HF = (cfg[k] for k in ("S", "E", "H", "D", "M", "NE", "HF"))
    R, NT, EC, NP, WBLK, PL, HFC = (cfg[k] for k in ("R", "NT", "EC", "NP", "WBLK", "PL", "HFC"))
    NPL = S // PL
    TPP = PL // 128
    NJ = (S + 511) // 512
    SW = min(512, S)
    HS = S // 2
    NJH = (HS + 511) // 512
    U1W = 4 * WBLK

    nc = bass.Bass("TRN2", target_bir_lowering=False)
    nc.detect_race_conditions = False

    dram = {}
    def din(name, shape):
        dram[name] = nc.dram_tensor(name, list(shape), F32, kind="ExternalInput").ap()
    din("xT", (E, S))
    din("Wq", (E, E)); din("Wk", (E, E)); din("Wv", (E, E)); din("Wo", (E, E))
    din("bq", (128, NP)); din("bk", (128, NP)); din("bv", (128, NP)); din("bo", (128, EC))
    din("gamma", (128, EC)); din("beta", (128, EC))
    din("eTx", (D, R + 2048)); din("erevTx", (D, R + 2048))
    din("identity", (128, 128)); din("onesm", (128, 128))
    din("WgT", (E, NE)); din("msel", (NE, NE * 128))
    din("lnsc", (NE * 128, EC)); din("lnbs", (NE * 128, EC))
    din("W1", (NE, E, HF)); din("b1c", (NE, 128, HFC))
    din("W2", (NE, HF, E)); din("b2", (NE, E))
    yT = nc.dram_tensor("yT", [E, S], F32, kind="ExternalOutput").ap()
    dbgA1 = None
    if os.environ.get("KDBG_A1"):
        dbgA1 = nc.dram_tensor("dbgA1", [128, 16 * 2048], mybir.dt.bfloat16, kind="ExternalOutput").ap()

    es = ExitStack()
    with es:
        def sbuf(name, shape, dt):
            return es.enter_context(nc.sbuf_tensor(name, list(shape), dt))

        A1 = sbuf("A1", [128, max(NT * S, 2 * EC * S)], BF16)
        A2 = sbuf("A2", [128, EC * S], BF16)
        A3 = sbuf("A3", [128, max(NP * S, NE * HFC * HS)], BF16)
        U1 = sbuf("U1", [128, U1W], BF16)
        A1f = A1.ap().bitcast(F32)
        U1f = U1.ap().bitcast(F32)
        xchunk = U1f[:, 0:2 * S]
        w2st = U1.ap()[:, 0:2 * NE * HFC * 128]
        eT_d = sbuf("eT_d", [128, R + 2048], BF16)
        erevT_d = sbuf("erevT_d", [128, R + 2048], BF16)
        identb = sbuf("identb", [128, 128], BF16)
        onesb = sbuf("onesb", [128, 64], BF16)
        identf = sbuf("identf", [128, 128], F32)
        onesf = sbuf("onesf", [128, 128], F32)
        mselb = sbuf("mselb", [8, NE * 128], BF16)
        qT_p = sbuf("qT_p", [128, S], BF16)
        kT_p = sbuf("kT_p", [128, S], BF16)
        vT_p = sbuf("vT_p", [128, S], BF16)
        vnat = sbuf("vnat", [128, NT * 65], BF16)
        Rpc = sbuf("Rpc", [128, 8 * PL], BF16)
        ychunk = Rpc.ap().bitcast(F32)[:, 0:2 * SW]
        sq_s = sbuf("sq_s", [128, 1024], F32)
        Mcol = sbuf("Mcol", [128, NT], F32)
        sc1 = sbuf("sc1", [1, 8], F32)
        cb = sbuf("cb", [128, 4], F32)
        mcol8 = sbuf("mcol8", [128, 8], F32)
        ROWS = sbuf("ROWS", [128, max(S, R)], F32)
        r_mu = ROWS[0:1, 0:S]
        r_rs = ROWS[32:33, 0:S]
        r_sv = ROWS[64:65, 0:S]
        r_mu2 = ROWS[96:97, 0:S]
        r_kn = ROWS[96:97, 0:S]
        r_emx = ROWS[96:97, 0:R]
        wst = sbuf("wst", [128, 3 * E], BF16)
        w1st = wst.ap()[:, 0:EC * HF]
        if S >= 2048:
            _o1 = 2 * WBLK + 2
            b2st = U1.ap()[0:8, _o1:_o1 + EC * 128]
        else:
            b2st = sbuf("b2st", [8, EC * 128], BF16).ap()[:, :]
        biascols = sbuf("biascols", [128, 3 * NP], F32)
        bocol = sbuf("bocol", [128, EC], F32)
        gcol = sbuf("gcol", [128, EC], F32)
        btcol = sbuf("btcol", [128, EC], F32)
        wgst = sbuf("wgst", [128, EC * NE], F32)
        if S >= 2048:
            _o2 = 2 * WBLK + 2 + EC * 128 + 2
            masks = U1.ap()[0:8, _o2:_o2 + S]
            _o3 = _o2 + S + 2
            lgch = U1.ap().bitcast(F32)[0:8, (_o3 + 1) // 2:(_o3 + 1) // 2 + SW]
        else:
            masks = sbuf("masks", [8, S], BF16).ap()[:, :]
            lgch = sbuf("lgch", [8, 512], F32).ap()[:, :]
        sccol = sbuf("sccol", [128, EC], F32)
        b1p = sbuf("b1p", [128, HFC], F32)
        lnbcol = sbuf("lnbcol", [128, EC], BF16)
        PS_A = es.enter_context(nc.psum_tensor("PS_A", [128, 1024], F32))
        PS_B = es.enter_context(nc.psum_tensor("PS_B", [128, 1024], F32))
        PS_C = es.enter_context(nc.psum_tensor("PS_C", [128, min(2048, max(S, 512))], F32))

        p = Prog(nc, es)
        p.mksem("g", "y", "t", "v", "s")
        GE, SY, VE, SE, TE = "gpsimd", "sync", "vector", "scalar", "tensor"

        for half in (0, 64):
            p.do(GE, lambda e, h=half: e.dma_start(out=eT_d[h:h + D, :], in_=dram["eTx"]), inc="g", n=16)
            p.do(GE, lambda e, h=half: e.dma_start(out=erevT_d[h:h + D, :], in_=dram["erevTx"]), inc="g", n=16)
        p.do(GE, lambda e: e.dma_start(out=identb[:], in_=dram["identity"]), inc="g", n=16)
        p.do(GE, lambda e: e.dma_start(out=onesb[:], in_=dram["onesm"][:, 0:64]), inc="g", n=16)
        p.do(GE, lambda e: e.dma_start(out=mselb[0:NE, :], in_=dram["msel"]), inc="g", n=16)
        p.do(SY, lambda e: e.dma_start(out=identf[:], in_=dram["identity"]), inc="y", n=16)
        p.do(SY, lambda e: e.dma_start(out=onesf[:], in_=dram["onesm"]), inc="y", n=16)
        for i, nm in enumerate(("bq", "bk", "bv")):
            p.do(SY, lambda e, s=dram[nm], i=i: e.dma_start(
                out=biascols[:, i * NP:(i + 1) * NP], in_=s), inc="y", n=16)
        p.do(SY, lambda e: e.dma_start(out=bocol[:], in_=dram["bo"]), inc="y", n=16)
        p.do(SY, lambda e: e.dma_start(out=gcol[:], in_=dram["gamma"]), inc="y", n=16)
        p.do(SY, lambda e: e.dma_start(out=btcol[:], in_=dram["beta"]), inc="y", n=16)
        tok_const_g = p.tok("g")
        tok_const_y = p.tok("y")

        p.wait(VE, tok_const_g)
        for i in range((R + 511) // 512):
            w = min(512, R - i * 512)
            p.do(VE, lambda e, i=i, w=w: e.tensor_tensor(
                out=sq_s[0:D, (i % 2) * 512:(i % 2) * 512 + w],
                in0=eT_d[0:D, 1024 + i * 512:1024 + i * 512 + w], in1=eT_d[0:D, 1024 + i * 512:1024 + i * 512 + w],
                op=ALU.mult), inc="v")
            vt = p.tok("v")
            p.wait(TE, vt); p.wait(TE, tok_const_y)
            p.do(TE, lambda e, i=i, w=w: e.matmul(PS_B[0:1, (i % 2) * 512:(i % 2) * 512 + w],
                 onesf[0:D, 0:1], sq_s[0:D, (i % 2) * 512:(i % 2) * 512 + w],
                 start=True, stop=True), inc="t")
            t = p.tok("t")
            p.wait(VE, t)
            p.do(VE, lambda e, i=i, w=w: e.tensor_copy(out=r_emx[0:1, i * 512:i * 512 + w],
                 in_=PS_B[0:1, (i % 2) * 512:(i % 2) * 512 + w]), inc="v")
            p.wait(TE, p.tok("v"))
        p.vsync()
        p.do(VE, lambda e: e.reduce_max(out=sc1[0:1, 0:1], in_=r_emx, axis=AX.X), inc="v")
        p.vsync()
        p.wait(SE, p.tok("v"))
        p.do(SE, lambda e: e.activation(out=sc1[0:1, 0:1], in_=sc1[0:1, 0:1], func=AF.Sqrt), inc="s")
        emx_tok = p.tok("s")

        xslot_reader = [None, None]
        def load_x_chunk(c, src):
            slot = c % 2
            rd = xslot_reader[slot]
            rds = () if rd is None else (rd if isinstance(rd[0], tuple) else (rd,))
            for t in rds:
                p.wait(SY, t)
            p.do(SY, lambda e, c=c, slot=slot, src=src: e.dma_start(
                out=xchunk[:, slot * S:(slot + 1) * S], in_=src[c * 128:(c + 1) * 128, :]), inc="y", n=16)
            return p.tok("y"), xchunk[:, slot * S:(slot + 1) * S], slot

        nsq = 4 if S >= 2048 else 2

        def sqap(sl, w):
            if sl < 2:
                return sq_s[:, sl * 512:sl * 512 + w]
            return Rpcf[:, 1024 + (sl - 2) * 512:1024 + (sl - 2) * 512 + w]

        def ln_stats(loader):
            prev_mm = None
            mm_slot = [None] * 4
            for c in range(EC):
                tk, apc, slot = loader(c)
                for i in range(NJ):
                    w = min(512, S - i * 512)
                    sl = i % nsq
                    if (c * NJ + i) % 2 == 0:
                        p.wait(SE, tk)
                        p.wait(SE, mm_slot[sl])
                        p.do(SE, lambda e, a=apc, i=i, w=w, sl=sl: e.activation(
                            out=sqap(sl, w), in_=a[:, i * 512:i * 512 + w],
                            func=AF.Square), inc="s")
                        stok = p.tok("s")
                    else:
                        p.wait(VE, tk)
                        p.wait(VE, mm_slot[sl])
                        p.do(VE, lambda e, a=apc, i=i, w=w, sl=sl: e.tensor_tensor(
                            out=sqap(sl, w), in0=a[:, i * 512:i * 512 + w],
                            in1=a[:, i * 512:i * 512 + w], op=ALU.mult), inc="v")
                        stok = p.tok("v")
                    p.wait(TE, tk); p.wait(TE, stok); p.wait(TE, tok_const_y)
                    p.do(TE, lambda e, a=apc, i=i, w=w, c=c: e.matmul(
                        PS_C[0:1, i * 512:i * 512 + w], onesf[:, 0:1],
                        a[:, i * 512:i * 512 + w], start=(c == 0), stop=(c == EC - 1)), inc="t")
                    p.do(TE, lambda e, i=i, w=w, c=c, sl=sl: e.matmul(
                        PS_C[32:33, i * 512:i * 512 + w], onesf[:, 0:1],
                        sqap(sl, w), start=(c == 0), stop=(c == EC - 1)), inc="t")
                    prev_mm = p.tok("t")
                    mm_slot[sl] = prev_mm
                if slot is not None:
                    xslot_reader[slot] = prev_mm
            return prev_mm

        def ln_rows(eps, comb=False):
            t = p.tok("t")
            p.wait(VE, t)
            p.do(VE, lambda e: e.tensor_scalar(out=r_mu, in0=PS_C[0:1, 0:S],
                 scalar1=1.0 / E, scalar2=None, op0=ALU.mult), inc="v")
            p.do(VE, lambda e: e.tensor_scalar(out=r_rs, in0=PS_C[32:33, 0:S],
                 scalar1=1.0 / E, scalar2=None, op0=ALU.mult), inc="v")
            p.vsync()
            p.do(VE, lambda e: e.tensor_tensor(out=PS_C[64:65, 0:S], in0=r_mu, in1=r_mu, op=ALU.mult), inc="v")
            p.do(VE, lambda e: e.tensor_tensor(out=r_rs, in0=r_rs, in1=PS_C[64:65, 0:S], op=ALU.subtract), inc="v")
            if comb:
                p.do(VE, lambda e: e.tensor_scalar(out=r_rs, in0=r_rs,
                     scalar1=1.0 + float(eps), scalar2=float(eps) * float(eps),
                     op0=ALU.mult, op1=ALU.add), inc="v")
            else:
                p.do(VE, lambda e: e.tensor_scalar(out=r_rs, in0=r_rs,
                     scalar1=float(eps), scalar2=None, op0=ALU.add), inc="v")
            vt = p.tok("v")
            p.wait(SE, vt)
            p.do(SE, lambda e: e.activation(out=r_rs, in_=r_rs, func=AF.Sqrt), inc="s")
            p.do(SE, lambda e: e.activation(out=r_rs, in_=r_rs, func=AF.Ln), inc="s")
            p.do(SE, lambda e: e.activation(out=r_rs, in_=r_rs, func=AF.Exp, scale=-1.0), inc="s")
            return p.tok("s")

        Rpcf = Rpc.ap().bitcast(F32)
        kT_pf = kT_p.ap().bitcast(F32)
        vT_pf = vT_p.ap().bitcast(F32)

        def muap(j, w):
            return Rpcf[:, j * 512:j * 512 + w]

        def rsap(j, w):
            if S <= 1024:
                return Rpcf[:, S + j * 512:S + j * 512 + w]
            return (kT_pf if j < 2 else vT_pf)[:, (j % 2) * 512:(j % 2) * 512 + w]

        def stage_rows(rows_tok):
            pvst = [None, None]
            for j in range(NJ):
                w = min(512, S - j * 512)
                sl = j % 2
                p.wait(TE, rows_tok); p.wait(TE, tok_const_y); p.wait(TE, pvst[sl])
                p.do(TE, lambda e, j=j, w=w, sl=sl: e.matmul(
                     PS_A[:, sl * 512:sl * 512 + w], onesf[0:1, :],
                     r_mu[0:1, j * 512:j * 512 + w], start=True, stop=True), inc="t")
                p.do(TE, lambda e, j=j, w=w, sl=sl: e.matmul(
                     PS_B[:, sl * 512:sl * 512 + w], onesf[32:33, :],
                     r_rs[0:1, j * 512:j * 512 + w], start=True, stop=True), inc="t")
                mt = p.tok("t")
                p.wait(VE, mt)
                p.do(VE, lambda e, j=j, w=w, sl=sl: e.tensor_copy(
                    out=muap(j, w), in_=PS_A[:, sl * 512:sl * 512 + w]), inc="v")
                p.do(VE, lambda e, j=j, w=w, sl=sl: e.tensor_copy(
                    out=rsap(j, w), in_=PS_B[:, sl * 512:sl * 512 + w]), inc="v")
                pvst[sl] = p.tok("v")
            return p.tok("v")

        def ln_apply(loader, out_ap_fn, stg):
            prev_ve = None
            for c in range(EC):
                tk, apc, slot = loader(c)
                for j in range(NJ):
                    w = min(512, S - j * 512)
                    p.wait(VE, tk); p.wait(VE, stg)
                    p.do(VE, lambda e, a=apc, j=j, w=w: e.tensor_tensor(
                        out=sq_s[:, 0:w], in0=a[:, j * 512:j * 512 + w],
                        in1=muap(j, w), op=ALU.subtract), inc="v")
                    p.do(VE, lambda e, c=c, j=j, w=w: e.tensor_tensor(
                        out=out_ap_fn(c, j, w), in0=sq_s[:, 0:w],
                        in1=rsap(j, w), op=ALU.mult), inc="v")
                    prev_ve = p.tok("v")
                if slot is not None:
                    xslot_reader[slot] = prev_ve
            return prev_ve

        ln_stats(lambda c: load_x_chunk(c, dram["xT"]))
        ln1_rows = ln_rows(1e-5)
        hsT_tok = ln_apply(lambda c: load_x_chunk(c, dram["xT"]),
                           lambda c, j, w: A2[:, c * S + j * 512:c * S + j * 512 + w],
                           stage_rows(ln1_rows))

        state = {"proj_read": None, "last_ctx_mm": None,
                 "last_exps": [None, None], "sl_ctx": [None] * 8}

        def projections(pair):
            p.wait(GE, state["proj_read"])
            wtoks = []
            for i, nm in enumerate(("Wq", "Wk", "Wv")):
                p.do(GE, lambda e, s=dram[nm], i=i, pr=pair: e.dma_start(
                    out=wst[:, i * E:(i + 1) * E],
                    in_=AP(s.tensor, pr * 128, [[E, 128], [128 * E, EC], [1, 128]])), inc="g", n=16)
                wtoks.append(p.tok("g"))
            outs = (qT_p, kT_p, vT_p)
            prev_ve = None
            pv = [None, None, None]
            for i in range(3):
                p.wait(TE, wtoks[i]); p.wait(TE, hsT_tok)
                p.wait(TE, state.get("last_norm_v"))
                for j in range(NJ):
                    w = min(512, S - j * 512)
                    pslot = (i * NJ + j) % 3
                    psl = (PS_B[:, 0:0 + w], PS_B[:, 512:512 + w], PS_A[:, 0:0 + w])[pslot]
                    p.wait(TE, pv[pslot])
                    for c in range(EC):
                        p.do(TE, lambda e, i=i, j=j, c=c, psl=psl, w=w: e.matmul(
                            psl,
                            wst[:, i * E + c * 128:i * E + c * 128 + 128],
                            A2[:, c * S + j * 512:c * S + j * 512 + w],
                            start=(c == 0), stop=(c == EC - 1)), inc="t")
                    mt = p.tok("t")
                    p.wait(VE, mt); p.wait(VE, tok_const_y)
                    if i == 0:
                        p.do(VE, lambda e, i=i, j=j, o=outs[i], psl=psl, pr=pair, w=w: e.tensor_scalar(
                            out=o[:, j * 512:j * 512 + w], in0=psl,
                            scalar1=biascols[:, i * NP + pr:i * NP + pr + 1], scalar2=0.125,
                            op0=ALU.add, op1=ALU.mult), inc="v")
                    else:
                        p.do(VE, lambda e, i=i, j=j, o=outs[i], psl=psl, pr=pair, w=w: e.tensor_scalar(
                            out=o[:, j * 512:j * 512 + w], in0=psl,
                            scalar1=biascols[:, i * NP + pr:i * NP + pr + 1], scalar2=None,
                            op0=ALU.add), inc="v")
                    prev_ve = p.tok("v")
                    pv[pslot] = prev_ve
            return prev_ve, p.tok("t")

        def head_attention(h, proj_tok):
            hb = 64 * (h % 2)
            pair = h // 2
            prev_ctx = state["last_ctx_mm"]

            p.wait(VE, proj_tok)
            if prev_ctx is not None:
                p.wait(VE, prev_ctx)
            p.do(VE, lambda e: e.memset(
                AP(vnat.ap().tensor, 64, [[NT * 65, 128], [65, NT]]), 1.0), inc="v")
            vz = p.tok("v")
            p.wait(VE, state.get("last_norm_t"))
            for i in range(NJ):
                w = min(512, S - i * 512)
                p.do(VE, lambda e, i=i, w=w: e.tensor_tensor(
                    out=A1[hb:hb + D, i * 512:i * 512 + w],
                    in0=kT_p[hb:hb + D, i * 512:i * 512 + w],
                    in1=kT_p[hb:hb + D, i * 512:i * 512 + w], op=ALU.mult), inc="v")
            for i in range(NJ):
                w = min(512, S - i * 512)
                p.do(VE, lambda e, i=i, w=w: e.tensor_tensor(
                    out=A1[hb:hb + D, S + i * 512:S + i * 512 + w],
                    in0=qT_p[hb:hb + D, i * 512:i * 512 + w],
                    in1=qT_p[hb:hb + D, i * 512:i * 512 + w], op=ALU.mult), inc="v")
            sq_tok = p.tok("v")
            p.wait(TE, vz); p.wait(TE, proj_tok); p.wait(TE, tok_const_g)
            idsl = identb[hb:hb + D, hb:hb + D]
            for rc in range(NT):
                p.do(TE, lambda e, rc=rc: e.matmul(
                    PS_B[:, rc * 64:rc * 64 + D],
                    vT_p[hb:hb + D, rc * 128:(rc + 1) * 128], idsl,
                    start=True, stop=True), inc="t")
            mt = p.tok("t")
            p.wait(VE, mt)
            p.do(VE, lambda e: e.tensor_copy(
                out=AP(vnat.ap().tensor, 0, [[NT * 65, 128], [65, NT], [1, D]]),
                in_=AP(PS_B.ap().tensor, 0, [[1024, 128], [64, NT], [1, D]])), inc="v")
            vnat_tok = p.tok("v")

            p.wait(TE, sq_tok)
            for i in range(NJ):
                w = min(512, S - i * 512)
                p.do(TE, lambda e, i=i, w=w: e.matmul(PS_C[0:1, i * 512:i * 512 + w],
                     onesb[hb:hb + D, 0:1], A1[hb:hb + D, i * 512:i * 512 + w],
                     start=True, stop=True), inc="t")
            kt = p.tok("t")
            p.wait(TE, state["last_exps"][1])
            for lt in range(NT):
                p.do(TE, lambda e, lt=lt: e.matmul(PS_A[:, 512 + lt:513 + lt],
                     A1[hb:hb + D, S + lt * 128:S + lt * 128 + 128], onesb[hb:hb + D, 0:1],
                     start=True, stop=True), inc="t")
            qn = p.tok("t")
            p.wait(VE, kt)
            p.do(VE, lambda e: e.tensor_copy(out=r_kn, in_=PS_C[0:1, 0:S]), inc="v")
            rkn_tok = p.tok("v")
            p.vsync()
            p.do(VE, lambda e: e.reduce_max(out=sc1[0:1, 1:2], in_=r_kn, axis=AX.X), inc="v")
            p.vsync()
            p.wait(SE, p.tok("v")); p.wait(SE, emx_tok)
            p.do(SE, lambda e: e.activation(out=sc1[0:1, 1:2], in_=sc1[0:1, 1:2], func=AF.Sqrt), inc="s")
            kmx = p.tok("s")
            p.wait(VE, kmx)
            p.do(VE, lambda e: e.tensor_tensor(out=sc1[0:1, 2:3], in0=sc1[0:1, 1:2], in1=sc1[0:1, 0:1], op=ALU.add), inc="v")
            p.do(VE, lambda e: e.tensor_tensor(out=sc1[0:1, 3:4], in0=sc1[0:1, 1:2], in1=sc1[0:1, 0:1], op=ALU.mult), inc="v")
            p.vsync()
            p.do(VE, lambda e: e.tensor_scalar(out=sc1[0:1, 2:3], in0=sc1[0:1, 2:3],
                 scalar1=1.01, scalar2=None, op0=ALU.mult), inc="v")
            p.do(VE, lambda e: e.tensor_scalar(out=sc1[0:1, 3:4], in0=sc1[0:1, 3:4],
                 scalar1=0.126, scalar2=0.5, op0=ALU.mult, op1=ALU.add), inc="v")
            p.vsync()
            c01 = p.tok("v")
            p.wait(TE, c01)
            p.do(TE, lambda e: e.matmul(PS_B[:, 0:1], onesf[0:1, :], sc1[0:1, 2:3], start=True, stop=True), inc="t")
            p.do(TE, lambda e: e.matmul(PS_B[:, 1:2], onesf[0:1, :], sc1[0:1, 3:4], start=True, stop=True), inc="t")
            cbt = p.tok("t")
            p.wait(VE, cbt)
            p.do(VE, lambda e: e.tensor_copy(out=cb[:, 0:2], in_=PS_B[:, 0:2]), inc="v")
            p.vsync()
            cb_tok = p.tok("v")
            p.wait(SE, qn)
            p.do(SE, lambda e: e.activation(out=Mcol[:, 0:NT], in_=PS_A[:, 512:512 + NT], func=AF.Sqrt), inc="s")
            sqt = p.tok("s")
            p.wait(VE, sqt); p.wait(VE, cb_tok)
            p.do(VE, lambda e: e.tensor_scalar(out=Mcol[:, 0:NT], in0=Mcol[:, 0:NT],
                 scalar1=cb[:, 0:1], scalar2=cb[:, 1:2], op0=ALU.mult, op1=ALU.add), inc="v")
            p.do(VE, lambda e: e.tensor_scalar(out=Mcol[:, 0:NT], in0=Mcol[:, 0:NT],
                 scalar1=-1.0, scalar2=None, op0=ALU.mult), inc="v")
            mcol_tok = p.tok("v")

            skew_toks = [None] * NT
            qd_evp = [sqt, cb_tok, rkn_tok, rkn_tok]
            npair = 4 if S >= 2048 else 2
            nch = (WBLK + 511) // 512
            pgc = [0]
            nuslot = 3 if NT > 3 else 2
            for lt in range(NT):
                slot = lt % nuslot
                base = slot * WBLK
                l0 = lt * 128
                c_lo = (M - 1) - 127 - l0
                ecol = 1024 + c_lo
                ev_se = None
                ev_ve = None
                ch = 0
                while ch < nch:
                    nsub = min(2, nch - ch)
                    bp = pgc[0] % npair
                    pgc[0] += 1
                    pst, poff = ((PS_A, 0), (PS_B, 0), (PS_C, 0), (PS_C, 1024))[bp]
                    ja = ch * 512
                    wtot = min(2 * 512, WBLK - ja) if nsub == 2 else min(512, WBLK - ja)
                    p.wait(TE, proj_tok); p.wait(TE, tok_const_g)
                    p.wait(TE, state.get("last_norm_v"))
                    p.wait(TE, qd_evp[bp])
                    for sub in range(nsub):
                        jas = (ch + sub) * 512
                        ws = min(512, WBLK - jas)
                        p.do(TE, lambda e, jas=jas, ws=ws, pst=pst, poff=poff, sub=sub,
                             ecol=ecol, l0=l0: e.matmul(
                            pst[:, poff + sub * 512:poff + sub * 512 + ws],
                            qT_p[hb:hb + D, l0:l0 + 128],
                            erevT_d[hb:hb + D, ecol + jas:ecol + jas + ws],
                            start=True, stop=True), inc="t")
                    mt = p.tok("t")
                    if pgc[0] % 2 == 0:
                        p.wait(SE, mt); p.wait(SE, mcol_tok)
                        if lt >= nuslot:
                            p.wait(SE, skew_toks[lt - nuslot])
                        if prev_ctx is not None:
                            p.wait(SE, prev_ctx)
                        p.do(SE, lambda e, ja=ja, wtot=wtot, pst=pst, poff=poff, base=base, lt=lt: e.activation(
                            out=U1.ap()[:, base + ja:base + ja + wtot],
                            in_=pst[:, poff:poff + wtot], func=AF.Identity,
                            bias=Mcol[:, lt:lt + 1]), inc="s")
                        qd_evp[bp] = ev_se = p.tok("s")
                    else:
                        p.wait(VE, mt); p.wait(VE, mcol_tok)
                        if lt >= nuslot:
                            p.wait(VE, skew_toks[lt - nuslot])
                        if prev_ctx is not None:
                            p.wait(VE, prev_ctx)
                        p.do(VE, lambda e, ja=ja, wtot=wtot, pst=pst, poff=poff, base=base, lt=lt: e.tensor_scalar(
                            out=U1.ap()[:, base + ja:base + ja + wtot],
                            in0=pst[:, poff:poff + wtot], scalar1=Mcol[:, lt:lt + 1], scalar2=None,
                            op0=ALU.add), inc="v")
                        qd_evp[bp] = ev_ve = p.tok("v")
                    ch += nsub
                Q, sem_n = (GE, "g") if lt % 2 == 0 else (SY, "y")
                p.wait(Q, ev_se); p.wait(Q, ev_ve)
                if prev_ctx is not None:
                    p.wait(Q, prev_ctx)
                p.do(Q, lambda e, base=base, lt=lt: e.dma_start(
                    out=A1[:, lt * S:(lt + 1) * S],
                    in_=AP(U1.ap().tensor, base + 127, [[U1W - 1, 128], [1, S]])), inc=sem_n, n=16)
                skew_toks[lt] = p.tok(sem_n)
            relq_tok = p.tok("g")
            if dbgA1 is not None and h == 0:
                p.wait(GE, relq_tok)
                p.do(GE, lambda e: e.dma_start(out=dbgA1[:, 0:NT * S], in_=A1[:, 0:NT * S]), inc="g", n=16)
                relq_tok = p.tok("g")

            nchk = (WBLK + 511) // 512
            kd_ev = [qd_evp[1], qd_evp[1]]
            kd_read = [(None, None), (None, None)]
            st_free = [qd_evp[0], qd_evp[0]]
            sl_exp = [None] * 8
            sl_ctx = state["sl_ctx"]
            r_toks = {}
            kdcnt = [0]

            def st_ps(stslot):
                return PS_A[:, stslot * PL:(stslot + 1) * PL]

            def emit_kd(rc):
                flushc(1)
                slot = rc % 2
                if nuslot == 3:
                    base = (3 * WBLK) if slot == 0 else 0
                else:
                    base = 2 * WBLK + slot * WBLK
                r0 = rc * 128
                ecol = 1024 + (M - 1) - 127 - r0
                for ch in range(nchk):
                    ja = ch * 512
                    w = min(512, WBLK - ja)
                    bank = kdcnt[0] % 2
                    kdcnt[0] += 1
                    p.wait(TE, proj_tok)
                    p.wait(TE, kd_ev[bank])
                    p.do(TE, lambda e, ja=ja, w=w, ecol=ecol, r0=r0, bank=bank: e.matmul(
                        PS_B[:, bank * 512:bank * 512 + w],
                        kT_p[hb:hb + D, r0:r0 + 128],
                        eT_d[hb:hb + D, ecol + ja:ecol + ja + w],
                        start=True, stop=True), inc="t")
                    mt = p.tok("t")
                    p.wait(VE, mt)
                    if nuslot == 3 and rc == 1:
                        for zt in [max((t for t in range(NT) if t % 3 == 0 and t % 2 == par), default=None)
                                   for par in (0, 1)]:
                            if zt is not None:
                                p.wait(VE, skew_toks[zt])
                    p.wait(VE, kd_read[slot][0]); p.wait(VE, kd_read[slot][1])
                    p.do(VE, lambda e, ja=ja, w=w, base=base, bank=bank: e.tensor_scalar(
                        out=U1.ap()[:, base + ja:base + ja + w],
                        in0=PS_B[:, bank * 512:bank * 512 + w],
                        scalar1=0.125, scalar2=None, op0=ALU.mult), inc="v")
                    kd_ev[bank] = p.tok("v")
                kd_blk = p.tok("v")
                last = [None, None]
                for lp in range(NPL):
                    rslot = (rc * NPL + lp) % 8
                    Q, sem_n, qi = (SY, "y", 0) if lp % 2 == 0 else (GE, "g", 1)
                    p.wait(Q, kd_blk)
                    p.wait(Q, sl_ctx[rslot])
                    p.do(Q, lambda e, base=base, lp=lp, rslot=rslot: e.dma_start(
                        out=Rpc[:, rslot * PL:(rslot + 1) * PL],
                        in_=AP(U1.ap().tensor, base + 127 + lp * PL,
                               [[U1W - 1, 128], [1, PL]])), inc=sem_n, n=16)
                    last[qi] = p.tok(sem_n)
                    r_toks[(rc, lp)] = (last[qi], None)
                kd_read[slot] = (last[0], last[1])

            pending = []

            def flushc(n):
                while len(pending) > n:
                    pending.pop(0)()
            emit_kd(0)
            for rc in range(NT):
                for lp in range(NPL):
                    idx = rc * NPL + lp
                    sslot = idx % 2
                    rslot = idx % 8
                    r_tok = r_toks.pop((rc, lp))
                    p.wait(TE, skew_toks[lp * TPP + TPP - 1])
                    if TPP >= 2:
                        p.wait(TE, skew_toks[lp * TPP + TPP - 2])
                    p.wait(TE, st_free[sslot])
                    p.wait(TE, r_tok[0]); p.wait(TE, r_tok[1])
                    p.do(TE, lambda e, rc=rc, lp=lp, sslot=sslot: e.matmul(
                        st_ps(sslot),
                        kT_p[hb:hb + D, rc * 128:rc * 128 + 128],
                        qT_p[hb:hb + D, lp * PL:lp * PL + PL],
                        start=True, stop=False), inc="t")
                    for k in range(TPP):
                        lt = lp * TPP + k
                        p.do(TE, lambda e, rc=rc, lt=lt, sslot=sslot, k=k: e.matmul(
                            st_ps(sslot)[:, k * 128:(k + 1) * 128],
                            A1[:, lt * S + rc * 128:lt * S + rc * 128 + 128],
                            identb[:], start=False, stop=False), inc="t")
                    p.do(TE, lambda e, sslot=sslot, rslot=rslot: e.matmul(
                        st_ps(sslot), identb[:],
                        Rpc[:, rslot * PL:(rslot + 1) * PL],
                        start=False, stop=True), inc="t")
                    st_mm = p.tok("t")
                    flushc(1)
                    p.wait(SE, st_mm)
                    p.do(SE, lambda e, sslot=sslot, rslot=rslot: e.activation(
                        out=Rpc[:, rslot * PL:(rslot + 1) * PL],
                        in_=st_ps(sslot), func=AF.Exp), inc="s")
                    sl_exp[rslot] = p.tok("s")
                    st_free[sslot] = sl_exp[rslot]

                    def mk_ctx(rc=rc, lp=lp, rslot=rslot, exp_tok=sl_exp[rslot]):
                        def doit():
                            p.wait(TE, exp_tok); p.wait(TE, vnat_tok)
                            p.do(TE, lambda e: e.matmul(
                                PS_C[0:65, lp * PL:lp * PL + PL],
                                vnat[:, rc * 65:(rc + 1) * 65],
                                Rpc[:, rslot * PL:(rslot + 1) * PL],
                                start=(rc == 0), stop=(rc == NT - 1)), inc="t")
                            sl_ctx[rslot] = p.tok("t")
                        return doit
                    pending.append(mk_ctx())
                    if lp == min(0, NPL - 1) and rc + 1 < NT:
                        emit_kd(rc + 1)
            flushc(0)
            ctx_mm = p.tok("t")
            state["last_ctx_mm"] = ctx_mm
            state["last_exps"] = [st_free[0], st_free[1]]

            p.wait(SE, ctx_mm)
            p.do(SE, lambda e: e.activation(out=r_sv, in_=PS_C[64:65, 0:S], func=AF.Ln), inc="s")
            p.do(SE, lambda e: e.activation(out=A1[64:65, 0:S], in_=r_sv, func=AF.Exp, scale=-1.0), inc="s")
            rsum_tok = p.tok("s")
            nv = [None, None]
            for j in range(NJ):
                w = min(512, S - j * 512)
                sl = j % 2
                p.wait(TE, rsum_tok); p.wait(TE, nv[sl])
                p.do(TE, lambda e, j=j, w=w, sl=sl: e.matmul(
                     PS_A[0:D, sl * 512:sl * 512 + w], onesb[64:65, 0:D],
                     A1[64:65, j * 512:j * 512 + w], start=True, stop=True), inc="t")
                bt = p.tok("t")
                p.wait(VE, bt)
                p.do(VE, lambda e, j=j, w=w, sl=sl: e.tensor_copy(
                    out=sq_s[0:D, sl * 512:sl * 512 + w], in_=PS_C[0:D, j * 512:j * 512 + w]), inc="v")
                p.do(VE, lambda e, j=j, w=w, sl=sl, pair=pair: e.tensor_tensor(
                    out=A3[hb:hb + D, pair * S + j * 512:pair * S + j * 512 + w],
                    in0=sq_s[0:D, sl * 512:sl * 512 + w], in1=PS_A[0:D, sl * 512:sl * 512 + w],
                    op=ALU.mult), inc="v")
                nv[sl] = p.tok("v")
            state["last_norm_t"] = p.tok("t")
            state["last_norm_v"] = p.tok("v")
            return p.tok("v")

        ctx_done = []
        for pr in range(NP):
            ptok, wread = projections(pr)
            state["proj_read"] = wread
            ctx_done.append(head_attention(2 * pr, ptok))
            ctx_done.append(head_attention(2 * pr + 1, ptok))

        for c in ctx_done:
            p.wait(TE, c)
        skl = (p.tok("y"), p.tok("g"))
        xslot_reader[0] = skl
        xslot_reader[1] = skl
        h_ev = [None, None, None, None]
        wo_rd = [None, None]
        for ec in range(EC):
            wslot = ec % 2
            p.wait(GE, state["proj_read"])
            p.wait(GE, wo_rd[wslot])
            p.do(GE, lambda e, ec=ec, wslot=wslot: e.dma_start(
                out=wst[:, wslot * E:(wslot + 1) * E],
                in_=AP(dram["Wo"].tensor, ec * 128, [[E, 128], [128 * E, EC], [1, 128]])), inc="g", n=16)
            wo_ld = p.tok("g")
            for j in range(NJ):
                w = min(512, S - j * 512)
                pslot = (ec * NJ + j) % 4
                psl = (PS_B[:, 0:0 + w], PS_B[:, 512:512 + w],
                       PS_A[:, 0:0 + w], PS_A[:, 512:512 + w])[pslot]
                p.wait(TE, wo_ld)
                p.wait(TE, h_ev[pslot])
                for kc in range(EC):
                    p.do(TE, lambda e, ec=ec, j=j, kc=kc, psl=psl, w=w, wslot=wslot: e.matmul(
                        psl,
                        wst[:, wslot * E + kc * 128:wslot * E + kc * 128 + 128],
                        A3[:, kc * S + j * 512:kc * S + j * 512 + w],
                        start=(kc == 0), stop=(kc == EC - 1)), inc="t")
                mt = p.tok("t")
                p.wait(VE, mt); p.wait(VE, hsT_tok); p.wait(VE, tok_const_y)
                p.do(VE, lambda e, ec=ec, psl=psl, w=w: e.tensor_scalar(
                    out=sq_s[:, 0:w], in0=psl,
                    scalar1=bocol[:, ec:ec + 1], scalar2=None, op0=ALU.add), inc="v")
                p.do(VE, lambda e, ec=ec, j=j, w=w: e.tensor_tensor(
                    out=A1f[:, ec * S + j * 512:ec * S + j * 512 + w],
                    in0=sq_s[:, 0:w], in1=A2[:, ec * S + j * 512:ec * S + j * 512 + w],
                    op=ALU.add), inc="v")
                h_ev[pslot] = p.tok("v")
            wo_rd[wslot] = p.tok("t")
        h_tok = p.tok("v")

        ln_stats(lambda c: (h_tok, A1f[:, c * S:(c + 1) * S], None))
        hrows = ln_rows(1e-12)
        stg3 = stage_rows(hrows)
        prev_ve = None
        for c in range(EC):
            xt, xap, slot = load_x_chunk(c, dram["xT"])
            for j in range(NJ):
                w = min(512, S - j * 512)
                p.wait(VE, stg3); p.wait(VE, xt)
                p.do(VE, lambda e, c=c, j=j, w=w: e.tensor_tensor(
                    out=sq_s[:, 0:w], in0=A1f[:, c * S + j * 512:c * S + j * 512 + w],
                    in1=muap(j, w), op=ALU.subtract), inc="v")
                p.do(VE, lambda e, j=j, w=w: e.tensor_tensor(
                    out=sq_s[:, 0:w], in0=sq_s[:, 0:w],
                    in1=rsap(j, w), op=ALU.mult), inc="v")
                p.do(VE, lambda e, c=c, w=w: e.scalar_tensor_tensor(
                    out=sq_s[:, 0:w], in0=sq_s[:, 0:w], scalar=gcol[:, c:c + 1],
                    in1=btcol[:, c:c + 1].to_broadcast([128, w]), op0=ALU.mult, op1=ALU.add), inc="v")
                p.do(VE, lambda e, c=c, j=j, w=w, slot=slot: e.tensor_tensor(
                    out=A1f[:, c * S + j * 512:c * S + j * 512 + w],
                    in0=sq_s[:, 0:w],
                    in1=xchunk[:, slot * S + j * 512:slot * S + j * 512 + w], op=ALU.add), inc="v")
                prev_ve = p.tok("v")
            xslot_reader[slot] = prev_ve
        x2_tok = prev_ve

        ln_stats(lambda c: (x2_tok, A1f[:, c * S:(c + 1) * S], None))
        ln2rows = ln_rows(1e-5, comb=True)
        stg2 = stage_rows(ln2rows)

        p.do(GE, lambda e: e.dma_start(
            out=wgst[:, :],
            in_=AP(dram["WgT"].tensor, 0, [[NE, 128], [128 * NE, EC], [1, NE]])), inc="g", n=16)
        wg_tok = p.tok("g")
        prev_gt = [None, None]
        lg_guard = None
        prev_mv = None
        for j in range(NJ):
            w = min(512, S - j * 512)
            for c in range(EC):
                zsl = c % 2
                p.wait(VE, stg2); p.wait(VE, x2_tok)
                p.wait(VE, prev_gt[zsl])
                p.do(VE, lambda e, c=c, j=j, w=w, zsl=zsl: e.tensor_tensor(
                    out=sq_s[:, zsl * 512:zsl * 512 + w],
                    in0=A1f[:, c * S + j * 512:c * S + j * 512 + w],
                    in1=muap(j, w), op=ALU.subtract), inc="v")
                z_ck = p.tok("v")
                p.wait(TE, z_ck); p.wait(TE, wg_tok)
                if c == 0:
                    p.wait(TE, lg_guard)
                p.do(TE, lambda e, c=c, w=w, zsl=zsl: e.matmul(
                    PS_B[0:NE, 0:w], wgst[:, c * NE:(c + 1) * NE],
                    sq_s[:, zsl * 512:zsl * 512 + w],
                    start=(c == 0), stop=(c == EC - 1)), inc="t")
                prev_gt[zsl] = p.tok("t")
            gt_last = p.tok("t")
            p.wait(VE, gt_last)
            p.do(VE, lambda e, w=w: e.tensor_copy(out=lgch[0:NE, 0:w], in_=PS_B[0:NE, 0:w]), inc="v")
            p.vsync()
            for ti in range(w // 128):
                lt = (j * 512) // 128 + ti
                p.wait(TE, p.tok("v")); p.wait(TE, tok_const_y)
                p.do(TE, lambda e, ti=ti: e.matmul(PS_B[0:128, 512:512 + NE],
                     lgch[0:NE, ti * 128:(ti + 1) * 128], identf[0:NE, 0:NE], start=True, stop=True), inc="t")
                tt = p.tok("t")
                p.wait(VE, tt)
                p.do(VE, lambda e: e.tensor_copy(out=mcol8[:, 0:NE], in_=PS_B[:, 512:512 + NE]), inc="v")
                p.vsync()
                p.do(VE, lambda e: e.reduce_max(out=cb[:, 2:3], in_=mcol8[:, 0:NE], axis=AX.X), inc="v")
                p.vsync()
                p.do(VE, lambda e: e.tensor_scalar(out=mcol8[:, 0:NE], in0=mcol8[:, 0:NE],
                     scalar1=cb[:, 2:3], scalar2=None, op0=ALU.is_ge), inc="v")
                p.do(VE, lambda e: e.memset(cb[:, 3:4], 0.0), inc="v")
                p.vsync()
                p.do(VE, lambda e: e.tensor_tensor_scan(out=sq_s[:, 0:NE], data0=mcol8[:, 0:NE],
                     data1=cb[:, 3:4].to_broadcast([128, NE]), initial=0.0, op0=ALU.add, op1=ALU.add), inc="v")
                p.vsync()
                p.do(VE, lambda e: e.tensor_scalar(out=sq_s[:, 0:NE], in0=sq_s[:, 0:NE],
                     scalar1=1.0, scalar2=None, op0=ALU.is_equal), inc="v")
                p.do(VE, lambda e: e.tensor_tensor(out=mcol8[:, 0:NE], in0=mcol8[:, 0:NE],
                     in1=sq_s[:, 0:NE], op=ALU.mult), inc="v")
                p.vsync()
                mcl = p.tok("v")
                p.wait(TE, mcl)
                p.do(TE, lambda e: e.matmul(PS_B[0:NE, 640:768],
                     mcol8[:, 0:NE], identf[:, 0:128], start=True, stop=True), inc="t")
                tb = p.tok("t")
                p.wait(VE, tb)
                p.do(VE, lambda e, lt=lt: e.tensor_copy(out=masks[0:NE, lt * 128:(lt + 1) * 128],
                     in_=PS_B[0:NE, 640:768]), inc="v")
                prev_mv = p.tok("v")
                p.wait(TE, prev_mv)
            lg_guard = p.tok("v")
        masks_tok = prev_mv
        xhat_tok = ln_apply(lambda c: (x2_tok, A1f[:, c * S:(c + 1) * S], None),
                            lambda c, j, w: A2[:, c * S + j * 512:c * S + j * 512 + w], stg2)

        p.wait(GE, masks_tok)
        p.do(GE, lambda e: e.dma_start(
            out=b2st[0:NE, :], in_=AP(dram["b2"].tensor, 0, [[E, NE], [1, E]])), inc="g", n=16)
        b2tok = p.tok("g")

        hm_last = None
        w1_read = None
        prev_b1tok = None
        prev_b1_user = None
        w2_rd = [None, None]
        ych_rd = [None, None]
        ffn_first = True
        for half in range(2):
            hoff = half * HS
            for ex in range(NE):
                p.wait(GE, w1_read); p.wait(GE, masks_tok)
                if ffn_first:
                    p.wait(GE, xhat_tok)
                    p.wait(GE, wo_rd[0]); p.wait(GE, wo_rd[1])
                p.do(GE, lambda e, ex=ex: e.dma_start(
                    out=w1st,
                    in_=AP(dram["W1"].tensor, ex * E * HF, [[HF, 128], [128 * HF, EC], [1, HF]])), inc="g", n=16)
                p.do(GE, lambda e, ex=ex: e.dma_start(
                    out=sccol[:, :], in_=dram["lnsc"][ex * 128:(ex + 1) * 128, :]), inc="g", n=16)
                p.do(GE, lambda e, ex=ex: e.dma_start(
                    out=lnbcol[:, :], in_=dram["lnbs"][ex * 128:(ex + 1) * 128, :]), inc="g", n=16)
                w1tok = p.tok("g")
                p.wait(TE, w1tok); p.wait(TE, prev_b1tok)
                for mc in range(HFC):
                    for c in range(EC):
                        p.do(TE, lambda e, mc=mc, c=c: e.matmul(
                            PS_C[0:128, mc:mc + 1],
                            w1st[:, c * HF + mc * 128:c * HF + (mc + 1) * 128],
                            lnbcol[:, c:c + 1], start=(c == 0), stop=(c == EC - 1)), inc="t")
                b1mm = p.tok("t")
                p.wait(VE, w1tok); p.wait(VE, b1mm)
                if hm_last is not None:
                    p.wait(VE, hm_last)
                for c in range(EC):
                    p.do(VE, lambda e, c=c: e.tensor_scalar(
                        out=w1st[:, c * HF:(c + 1) * HF], in0=w1st[:, c * HF:(c + 1) * HF],
                        scalar1=sccol[:, c:c + 1], scalar2=None, op0=ALU.mult), inc="v")
                w1s_tok = p.tok("v")
                p.wait(SY, prev_b1_user)
                p.do(SY, lambda e, ex=ex: e.dma_start(
                    out=b1p[:, 0:HFC],
                    in_=AP(dram["b1c"].tensor, ex * 128 * HFC, [[HFC, 128], [1, HFC]])), inc="y", n=16)
                b1ld = p.tok("y")
                p.wait(VE, b1mm); p.wait(VE, b1ld)
                p.do(VE, lambda e: e.tensor_tensor(out=b1p[:, 0:HFC], in0=b1p[:, 0:HFC],
                     in1=PS_C[:, 0:HFC], op=ALU.add), inc="v")
                p.vsync()
                b1tok = p.tok("v")
                prev_b1tok = b1tok
                hbase = ex * HFC * HS
                he_mm = [None, None]
                mb_rd = [None, None]
                for mc in range(HFC):
                    for j in range(NJH):
                        w = min(512, HS - j * 512)
                        pslot = (mc * NJH + j) % 2
                        gj = hoff + j * 512
                        p.wait(TE, xhat_tok); p.wait(TE, w1s_tok)
                        p.wait(TE, he_mm[pslot])
                        for c in range(EC):
                            p.do(TE, lambda e, mc=mc, gj=gj, c=c, pslot=pslot, w=w: e.matmul(
                                PS_A[:, pslot * 512:pslot * 512 + w],
                                w1st[:, c * HF + mc * 128:c * HF + (mc + 1) * 128],
                                A2[:, c * S + gj:c * S + gj + w],
                                start=(c == 0), stop=(c == EC - 1)), inc="t")
                        p.wait(TE, masks_tok); p.wait(TE, tok_const_g)
                        p.wait(TE, mb_rd[pslot])
                        p.do(TE, lambda e, ex=ex, gj=gj, pslot=pslot, w=w: e.matmul(
                            PS_B[:, pslot * 512:pslot * 512 + w],
                            mselb[0:NE, ex * 128:(ex + 1) * 128],
                            masks[0:NE, gj:gj + w], start=True, stop=True), inc="t")
                        hmm = p.tok("t")
                        p.wait(SE, hmm); p.wait(SE, b1tok)
                        if mb_rd[pslot] is not None:
                            p.wait(SE, mb_rd[pslot])
                        p.do(SE, lambda e, mc=mc, pslot=pslot, w=w: e.activation(
                            out=sq_s[:, pslot * 512:pslot * 512 + w], in_=PS_A[:, pslot * 512:pslot * 512 + w],
                            func=AF.Gelu, bias=b1p[:, mc:mc + 1]), inc="s")
                        ge_tok = p.tok("s")
                        he_mm[pslot] = ge_tok
                        prev_b1_user = ge_tok
                        p.wait(VE, ge_tok)
                        if ffn_first:
                            p.wait(VE, wo_rd[0]); p.wait(VE, wo_rd[1])
                        p.do(VE, lambda e, mc=mc, j=j, pslot=pslot, w=w, hbase=hbase: e.tensor_tensor(
                            out=A3[:, hbase + mc * HS + j * 512:hbase + mc * HS + j * 512 + w],
                            in0=sq_s[:, pslot * 512:pslot * 512 + w], in1=PS_B[:, pslot * 512:pslot * 512 + w],
                            op=ALU.mult), inc="v")
                        mb_rd[pslot] = p.tok("v")
                        hm_last = mb_rd[pslot]
                        ffn_first = False
                w1_read = p.tok("t")
            hall_tok = hm_last

            for ec in range(EC):
                wslot = ec % 2
                p.wait(GE, w2_rd[wslot]); p.wait(GE, hall_tok)
                p.do(GE, lambda e, ec=ec, wslot=wslot: e.dma_start(
                    out=w2st[:, wslot * NE * HFC * 128:(wslot + 1) * NE * HFC * 128],
                    in_=AP(dram["W2"].tensor, ec * 128, [[E, 128], [HF * E, NE], [128 * E, HFC], [1, 128]])), inc="g", n=16)
                w2ld = p.tok("g")
                for j in range(NJH):
                    w = min(512, HS - j * 512)
                    gj = hoff + j * 512
                    pslot = j % 2
                    p.wait(TE, hall_tok); p.wait(TE, w2ld); p.wait(TE, b2tok); p.wait(TE, masks_tok)
                    p.wait(TE, ych_rd[pslot])
                    first = True
                    for ex in range(NE):
                        hbase = ex * HFC * HS
                        for kc in range(HFC):
                            p.do(TE, lambda e, j=j, ex=ex, kc=kc, pslot=pslot, w=w, first=first,
                                 hbase=hbase, wslot=wslot: e.matmul(
                                PS_A[:, pslot * 512:pslot * 512 + w],
                                w2st[:, wslot * NE * HFC * 128 + (ex * HFC + kc) * 128:
                                     wslot * NE * HFC * 128 + (ex * HFC + kc) * 128 + 128],
                                A3[:, hbase + kc * HS + j * 512:hbase + kc * HS + j * 512 + w],
                                start=first, stop=False), inc="t")
                            first = False
                    p.do(TE, lambda e, ec=ec, gj=gj, pslot=pslot, w=w: e.matmul(
                        PS_A[:, pslot * 512:pslot * 512 + w],
                        b2st[0:NE, ec * 128:(ec + 1) * 128],
                        masks[0:NE, gj:gj + w], start=False, stop=True), inc="t")
                    ymm = p.tok("t")
                    p.wait(VE, ymm); p.wait(VE, x2_tok)
                    p.wait(VE, ych_rd[pslot])
                    p.do(VE, lambda e, ec=ec, gj=gj, pslot=pslot, w=w: e.tensor_tensor(
                        out=ychunk[:, pslot * SW:pslot * SW + w],
                        in0=PS_A[:, pslot * 512:pslot * 512 + w],
                        in1=A1f[:, ec * S + gj:ec * S + gj + w], op=ALU.add), inc="v")
                    yev = p.tok("v")
                    p.wait(SY, yev)
                    p.do(SY, lambda e, ec=ec, gj=gj, pslot=pslot, w=w: e.dma_start(
                        out=yT[ec * 128:(ec + 1) * 128, gj:gj + w],
                        in_=ychunk[:, pslot * SW:pslot * SW + w]), inc="y", n=16)
                    ych_rd[pslot] = p.tok("y")
                w2_rd[wslot] = p.tok("t")
        p.wait(SY, p.tok("y"))

        block = es.enter_context(nc.Block())

        @block.sync
        def _(sync):
            for f in p.ops["sync"]:
                f(sync)

        @block.gpsimd
        def _(gpsimd):
            for f in p.ops["gpsimd"]:
                f(gpsimd)

        @block.vector
        def _(vector):
            for f in p.ops["vector"]:
                f(vector)

        @block.scalar
        def _(scalar):
            for f in p.ops["scalar"]:
                f(scalar)

        @block.tensor
        def _(tensor):
            for f in p.ops["tensor"]:
                f(tensor)

    return nc


def host_inputs(cfg, inputs, b):
    S, E, H, M, NE, HF = (cfg[k] for k in ("S", "E", "H", "M", "NE", "HF"))
    x = np.ascontiguousarray(np.asarray(inputs["L0_attn_hidden_states"])[b].T)
    d = {
        "xT": x,
        "Wq": np.ascontiguousarray(inputs["L0_attn_Wq"]),
        "Wk": np.ascontiguousarray(inputs["L0_attn_Wk"]),
        "Wv": np.ascontiguousarray(inputs["L0_attn_Wv"]),
        "Wo": np.ascontiguousarray(inputs["L0_attn_Wo"]),
        "bq": np.ascontiguousarray(np.asarray(inputs["L0_attn_bq"]).reshape(-1, 128).T),
        "bk": np.ascontiguousarray(np.asarray(inputs["L0_attn_bk"]).reshape(-1, 128).T),
        "bv": np.ascontiguousarray(np.asarray(inputs["L0_attn_bv"]).reshape(-1, 128).T),
        "bo": np.ascontiguousarray(np.asarray(inputs["L0_attn_bo"]).reshape(-1, 128).T),
        "gamma": np.ascontiguousarray(np.asarray(inputs["L0_attn_ln_gamma"]).reshape(-1, 128).T),
        "beta": np.ascontiguousarray(np.asarray(inputs["L0_attn_ln_beta"]).reshape(-1, 128).T),
        "eTx": np.ascontiguousarray(np.pad(np.asarray(inputs["L0_attn_dist_emb"]).T, ((0, 0), (1024, 1024)), mode="edge")),
        "erevTx": np.ascontiguousarray(np.pad(np.asarray(inputs["L0_attn_dist_emb"])[::-1].T, ((0, 0), (1024, 1024)), mode="edge")),
        "identity": np.eye(128, dtype=np.float32),
        "onesm": np.ones((128, 128), dtype=np.float32),
        "msel": np.repeat(np.eye(NE, dtype=np.float32), 128, axis=1),
        "WgT": np.ascontiguousarray(np.asarray(inputs["L0_ffn_Wg"]).T),
        "lnsc": np.ascontiguousarray(np.asarray(inputs["L0_ffn_ln_scale"]).reshape(NE, -1, 128).transpose(0, 2, 1).reshape(NE * 128, -1)),
        "lnbs": np.ascontiguousarray(np.asarray(inputs["L0_ffn_ln_bias"]).reshape(NE, -1, 128).transpose(0, 2, 1).reshape(NE * 128, -1)),
        "W1": np.ascontiguousarray(inputs["L0_ffn_W1"]),
        "b1c": np.ascontiguousarray(
            np.asarray(inputs["L0_ffn_b1"]).reshape(NE, HF // 128, 128).transpose(0, 2, 1)),
        "W2": np.ascontiguousarray(inputs["L0_ffn_W2"]),
        "b2": np.ascontiguousarray(inputs["L0_ffn_b2"]),
    }
    return {k: np.asarray(v, dtype=np.float32) for k, v in d.items()}


def kernel(**inputs):
    cfg = make_cfg()
    nc = build_program(cfg)
    in_maps = [host_inputs(cfg, inputs, b) for b in range(8)]
    res = run_bass_kernel_spmd(nc, in_maps, core_ids=list(range(8)))
    outs = [np.asarray(r["yT"]).T for r in res.results]
    return np.stack(outs, axis=0).astype(np.float32)
